# version 15
# baseline (speedup 1.0000x reference)
import contextlib
import math
import numpy as np
import concourse.bass as bass
import concourse.mybir as mybir
from concourse.bass_utils import run_bass_kernel_spmd

F32 = mybir.dt.float32
BF16 = mybir.dt.bfloat16
AF = mybir.ActivationFunctionType
ALU = mybir.AluOpType

TOK = 8192
DM = 2048
OWN = 1024
NCORE = 8
LAT = 512
H = 16
HD = 128
PD = 64
FC = DM // 128
LC = LAT // 128
NG = TOK // 512
OG = OWN // 512
E = 8
IM = 1408
ICE = IM // 128
NSUB = 4
SCALE = 1.0 / math.sqrt(HD + PD)
STAB = -4.0
EPS = 1e-5
X = mybir.AxisListType.X


def build_program(stage="full", debug=(), nheads=H, nexp=E + NSUB, precast=True):
    nc = bass.Bass("TRN2", target_bir_lowering=False)
    dbg = set(debug)

    x_d = nc.dram_tensor("x", [TOK, DM], F32, kind="ExternalInput").ap()
    tabk_d = nc.dram_tensor("tabk", [2, PD, TOK], F32, kind="ExternalInput").ap()
    cosq_d = nc.dram_tensor("cosq", [PD, OWN], F32, kind="ExternalInput").ap()
    sinq_d = nc.dram_tensor("sinq", [PD, OWN], F32, kind="ExternalInput").ap()
    wdkv_d = nc.dram_tensor("wdkv", [128, FC * LC * 128], F32, kind="ExternalInput").ap()
    wdq_d = nc.dram_tensor("wdq", [128, FC * LC * 128], F32, kind="ExternalInput").ap()
    wkr_d = nc.dram_tensor("wkr", [128, FC * PD], F32, kind="ExternalInput").ap()
    wuk_d = nc.dram_tensor("wuk", [128, LC * DM], F32, kind="ExternalInput").ap()
    wuv_d = nc.dram_tensor("wuv", [128, LC * DM], F32, kind="ExternalInput").ap()
    wuq_d = nc.dram_tensor("wuq", [128, LC * DM], F32, kind="ExternalInput").ap()
    wqr_d = nc.dram_tensor("wqr", [128, LC * H * PD], F32, kind="ExternalInput").ap()
    wo_d = nc.dram_tensor("wo", [128, FC * DM], F32, kind="ExternalInput").ap()
    rtr_d = nc.dram_tensor("rtr", [128, FC * E], F32, kind="ExternalInput").ap()
    sgw_d = nc.dram_tensor("sgw", [128, FC], F32, kind="ExternalInput").ap()
    gu_d = nc.dram_tensor("gu", [E + NSUB, ICE, 2, 128, DM], F32, kind="ExternalInput").ap()
    dn_d = nc.dram_tensor("dn", [E + NSUB, ICE, 128, DM], F32, kind="ExternalInput").ap()
    out_d = nc.dram_tensor("out", [OWN, DM], F32, kind="ExternalOutput").ap()

    gu_bf = nc.dram_tensor("gu_bf", [E + NSUB, ICE, 2, 128, DM], BF16,
                           kind="ExternalOutput" if "gubf" in dbg else "Internal").ap()
    dn_bf = nc.dram_tensor("dn_bf", [E + NSUB, ICE, 128, DM], BF16,
                           kind="ExternalOutput" if "gubf" in dbg else "Internal").ap()
    wo_bf = nc.dram_tensor("wo_bf", [128, FC * DM], BF16, kind="Internal").ap()
    xn_sc = nc.dram_tensor("xn_sc", [TOK, DM], BF16,
                           kind="ExternalOutput" if "xn" in dbg else "Internal").ap()
    ckv_sc = nc.dram_tensor("ckv_sc", [NG, 128, LC * 512], BF16,
                            kind="ExternalOutput" if "ckv" in dbg else "Internal").ap()
    o_sc = nc.dram_tensor("o_sc", [OWN, H * HD], BF16,
                          kind="ExternalOutput" if "osc" in dbg else "Internal").ap()
    if "krt" in dbg:
        krt_dbg = nc.dram_tensor("krt_dbg", [PD, TOK], BF16, kind="ExternalOutput").ap()
    if "cqt" in dbg:
        cqt_dbg = nc.dram_tensor("cqt_dbg", [128, LC * OWN], BF16, kind="ExternalOutput").ap()
    if "kvq" in dbg:
        kct_dbg = nc.dram_tensor("kct_dbg", [128, TOK], BF16, kind="ExternalOutput").ap()
        vext_dbg = nc.dram_tensor("vext_dbg", [128, 64 * 129], BF16, kind="ExternalOutput").ap()
        qct_dbg = nc.dram_tensor("qct_dbg", [128, OWN], BF16, kind="ExternalOutput").ap()
        qrt_dbg = nc.dram_tensor("qrt_dbg", [PD, OWN], BF16, kind="ExternalOutput").ap()
        expt_dbg = nc.dram_tensor("expt_dbg", [128, 1024], BF16, kind="ExternalOutput").ap()
    if "ctx0" in dbg:
        ctx0_dbg = nc.dram_tensor("ctx0_dbg", [128, 512], BF16, kind="ExternalOutput").ap()
    if "oacc" in dbg:
        oacc_dbg = nc.dram_tensor("oacc_dbg", [128, 8 * DM], F32, kind="ExternalOutput").ap()
        ot_dbg = nc.dram_tensor("ot_dbg", [128, FC * OWN], BF16, kind="ExternalOutput").ap()
        ot2_dbg = nc.dram_tensor("ot2_dbg", [128, FC * OWN], BF16, kind="ExternalOutput").ap()
        wosb_dbg = nc.dram_tensor("wosb_dbg", [128, FC * DM], BF16, kind="ExternalOutput").ap()
        xr_dbg = nc.dram_tensor("xr_dbg", [128, DM], F32, kind="ExternalOutput").ap()
    if "ffn0" in dbg:
        ht_dbg = nc.dram_tensor("ht_dbg", [128, ICE * OWN], BF16, kind="ExternalOutput").ap()
        gub_dbg = nc.dram_tensor("gub_dbg", [2, 128, 2 * DM], BF16, kind="ExternalOutput").ap()
        dnt_dbg = nc.dram_tensor("dnt_dbg", [128, ICE * DM], BF16, kind="ExternalOutput").ap()
        silu_dbg = nc.dram_tensor("silu_dbg", [128, 1024], F32, kind="ExternalOutput").ap()
    if "comb" in dbg:
        comb_dbg = nc.dram_tensor("comb_dbg", [128, 8 * E], F32, kind="ExternalOutput").ap()
        sg_dbg = nc.dram_tensor("sg_dbg", [128, 8], F32, kind="ExternalOutput").ap()

    cnt = {}

    def inc(ins, s, n):
        cnt[s.name] = cnt.get(s.name, 0) + n
        ins.then_inc(s, n)
        return cnt[s.name]

    def tot(s):
        return cnt.get(s.name, 0)

    def ilk(eng, ins, s):
        c = inc(ins, s, 1)
        eng.wait_ge(s, c)
        return c

    with contextlib.ExitStack() as _st:
        EC = _st.enter_context
        sX = EC(nc.semaphore("sX"))
        sXNW = EC(nc.semaphore("sXNW"))
        sXNT = EC(nc.semaphore("sXNT"))
        sCKW = EC(nc.semaphore("sCKW"))
        sCKR = EC(nc.semaphore("sCKR"))
        sW = EC(nc.semaphore("sW"))
        sOS = EC(nc.semaphore("sOS"))
        sOT = EC(nc.semaphore("sOT"))
        sOUT = EC(nc.semaphore("sOUT"))
        sPC = EC(nc.semaphore("sPC"))
        sXL = [EC(nc.semaphore("sXL%d" % i)) for i in range(8)]
        sTB = [EC(nc.semaphore("sTB%d" % i)) for i in range(2)]
        sXT = [EC(nc.semaphore("sXT%d" % i)) for i in range(2)]
        sCK = [EC(nc.semaphore("sCK%d" % i)) for i in range(5)]
        sGU = [EC(nc.semaphore("sGU%d" % i)) for i in range(2)]
        sDN = EC(nc.semaphore("sDN"))
        sMM = EC(nc.semaphore("sMM"))
        sVE = EC(nc.semaphore("sVE"))
        sAC = EC(nc.semaphore("sAC"))
        xnT_own = EC(nc.sbuf_tensor("xnT_own", [128, FC * OWN], BF16))
        _pk = contextlib.ExitStack()
        _st.enter_context(_pk)
        krt_sb = _pk.enter_context(nc.sbuf_tensor("krt_sb", [PD, TOK], BF16))
        S0 = EC(nc.psum_tensor("S0", [128, 1024], F32))
        S1 = EC(nc.psum_tensor("S1", [128, 1024], F32))
        C0 = EC(nc.psum_tensor("C0", [128, 512], F32))
        C1 = EC(nc.psum_tensor("C1", [128, 512], F32))
        C2 = EC(nc.psum_tensor("C2", [128, 512], F32))
        C3 = EC(nc.psum_tensor("C3", [128, 512], F32))
        block = EC(nc.Block())
        1
        CT = [C0, C1, C2, C3]
        SS = [S0, S1]
        s0_half_guard = [0, 0]

        _p0 = contextlib.ExitStack()
        _st.enter_context(_p0)
        if True:
            E0 = _p0.enter_context
            ckg0 = E0(nc.sbuf_tensor("ckg0", [PD, 1024], F32))
            ckg1 = E0(nc.sbuf_tensor("ckg1", [PD, 1024], F32))
            xbufs = [E0(nc.sbuf_tensor("xbuf%d" % i, [128, DM], F32)) for i in range(8)]
            xnb0 = E0(nc.sbuf_tensor("xnb0", [128, DM], BF16))
            xnb1 = E0(nc.sbuf_tensor("xnb1", [128, DM], BF16))
            xnb2 = E0(nc.sbuf_tensor("xnb2", [128, DM], BF16))
            xnb3 = E0(nc.sbuf_tensor("xnb3", [128, DM], BF16))
            trash_sb = E0(nc.sbuf_tensor("trash_sb", [128, DM], BF16))
            st_sum = E0(nc.sbuf_tensor("st_sum", [128, 8], F32))
            st_ssq = E0(nc.sbuf_tensor("st_ssq", [128, 8], F32))
            st_m = E0(nc.sbuf_tensor("st_m", [128, 8], F32))
            st_negm = E0(nc.sbuf_tensor("st_negm", [128, 8], F32))
            st_msq = E0(nc.sbuf_tensor("st_msq", [128, 8], F32))
            st_var = E0(nc.sbuf_tensor("st_var", [128, 8], F32))
            st_rstd = E0(nc.sbuf_tensor("st_rstd", [128, 8], F32))
            st_nmr = E0(nc.sbuf_tensor("st_nmr", [128, 8], F32))
            xnTg0 = E0(nc.sbuf_tensor("xnTg0", [128, FC * 512], BF16))
            xnTg1 = E0(nc.sbuf_tensor("xnTg1", [128, FC * 512], BF16))
            wdkv_sb = E0(nc.sbuf_tensor("wdkv_sb", [128, FC * LC * 128], BF16))
            wkr_sb = E0(nc.sbuf_tensor("wkr_sb", [128, FC * PD], BF16))
            wkr_rot = E0(nc.sbuf_tensor("wkr_rot", [128, FC * PD], BF16))
            ckvsts = [E0(nc.sbuf_tensor("ckvst%d" % i, [128, 512], BF16)) for i in range(4)]
            rtmpa = E0(nc.sbuf_tensor("rtmpa", [PD, 512], F32))
            rtmpb = E0(nc.sbuf_tensor("rtmpb", [PD, 512], F32))
            xnbufs = [xnb0, xnb1, xnb2, xnb3]
            xnTgs = [xnTg0, xnTg1]

            @block.gpsimd
            def _(g):
                inc(g.dma_start(wdkv_sb[:], wdkv_d), sW, 16)
                inc(g.dma_start(wkr_sb[:], wkr_d), sW, 16)

            @block.vector
            def _(v):
                v.wait_ge(sW, 32)
                wv = wkr_sb[:].rearrange("p (c d) -> p c d", d=PD)
                rv = wkr_rot[:].rearrange("p (c d) -> p c d", d=PD)
                v.tensor_scalar(rv[:, :, 0:32], wv[:, :, 32:64], -1.0, None, ALU.mult)
                i = v.tensor_scalar(rv[:, :, 32:64], wv[:, :, 0:32], 1.0, None, ALU.mult)
                inc(i, sVE, 1)
            wkr_rot_done = tot(sVE)

            aff_done = [0] * 64
            xnw_done = [0] * 64
            xnt_done = [0] * NG
            mm_grp_done = [0] * NG
            s1_guard = 0
            ckw_stage = {}

            for t in range(64):
                g = t // 4
                t4 = t % 4
                pp = (g % 2) * 4 + t4

                @block.sync
                def _(sy, t=t):
                    if t >= 8:
                        sy.wait_ge(sAC, aff_done[t - 8])
                    i = sy.dma_start(xbufs[t % 8][:], x_d[t * 128:(t + 1) * 128, :])
                    inc(i, sXL[t % 8], 16)

                @block.vector
                def _(v, t=t, pp=pp):
                    v.wait_ge(sXL[t % 8], 16 * (t // 8 + 1))
                    v.reduce_sum(st_sum[:, pp:pp + 1], xbufs[t % 8][:], axis=X)
                    i = v.scalar_tensor_tensor(trash_sb[:], xbufs[t % 8][:], 1.0,
                                               xbufs[t % 8][:], ALU.mult, ALU.mult,
                                               accum_out=st_ssq[:, pp:pp + 1])
                    inc(i, sVE, 1)

                if t % 4 != 3:
                    continue

                q0 = (g % 2) * 4
                P4 = slice(q0, q0 + 4)

                @block.vector
                def _(v, g=g, P4=P4):
                    ilk(v, v.tensor_scalar(st_m[:, P4], st_sum[:, P4], 1.0 / DM, None, ALU.mult), sVE)
                    ilk(v, v.tensor_scalar(st_negm[:, P4], st_m[:, P4], -1.0, None, ALU.mult), sVE)
                    ilk(v, v.tensor_tensor(st_msq[:, P4], st_m[:, P4], st_m[:, P4], ALU.mult), sVE)
                    ilk(v, v.tensor_scalar(st_var[:, P4], st_ssq[:, P4], 1.0 / DM, None, ALU.mult), sVE)
                    ilk(v, v.tensor_sub(st_var[:, P4], st_var[:, P4], st_msq[:, P4]), sVE)
                    ilk(v, v.tensor_scalar(st_var[:, P4], st_var[:, P4], EPS, None, ALU.add), sVE)
                    i = v.reciprocal(st_var[:, P4], st_var[:, P4])
                    cnt["ln_chain_%d" % g] = inc(i, sVE, 1)

                @block.scalar
                def _(s, g=g, P4=P4, q0=q0):
                    s.wait_ge(sVE, cnt["ln_chain_%d" % g])
                    i = s.activation(st_rstd[:, P4], st_var[:, P4], AF.Sqrt)
                    inc(i, sAC, 1)
                    s.wait_ge(sAC, tot(sAC))
                    for t4 in range(4):
                        t = 4 * g + t4
                        pp = q0 + t4
                        s.wait_ge(sAC, tot(sAC))
                        i = s.activation(st_nmr[:, pp:pp + 1], st_negm[:, pp:pp + 1],
                                         AF.Identity, bias=0.0, scale=st_rstd[:, pp:pp + 1])
                        inc(i, sAC, 1)
                    s.wait_ge(sAC, tot(sAC))
                    for t4 in range(4):
                        t = 4 * g + t4
                        pp = q0 + t4
                        if t >= 4:
                            s.wait_ge(sXNW, xnw_done[t - 1])
                        i = s.activation(xnbufs[t4][:], xbufs[t % 8][:], AF.Identity,
                                         bias=st_nmr[:, pp:pp + 1], scale=st_rstd[:, pp:pp + 1])
                        aff_done[t] = inc(i, sAC, 1)
                        s.wait_ge(sAC, aff_done[t])
                        i = s.dma_start(xn_sc[t * 128:(t + 1) * 128, :], xnbufs[t4][:])
                        xnw_done[t] = inc(i, sXNW, 16)

                gb = g % 2
                own_group = g < OG

                @block.scalar
                def _(sy, g=g, gb=gb, own_group=own_group):
                    if not own_group and g >= 4:
                        sy.wait_ge(sMM, mm_grp_done[g - 2])
                    sy.wait_ge(sXNW, xnw_done[4 * g + 3])
                    for f in range(FC):
                        if own_group:
                            dst = xnT_own[:, f * OWN + g * 512: f * OWN + (g + 1) * 512]
                        else:
                            dst = xnTgs[gb][:, f * 512:(f + 1) * 512]
                        i = sy.dma_start_transpose(
                            dst, xn_sc[g * 512:(g + 1) * 512, f * 128:(f + 1) * 128])
                        xnt_done[g] = inc(i, sXT[g % 2], 16)

                def rhs_of(f, g=g, gb=gb, own_group=own_group):
                    if own_group:
                        return xnT_own[:, f * OWN + g * 512: f * OWN + (g + 1) * 512]
                    return xnTgs[gb][:, f * 512:(f + 1) * 512]

                def emit_evac(ge):
                    @block.vector
                    def _(v, ge=ge):
                        if ge >= 1:
                            v.wait_ge(sCKW, 64 * ge)
                        for lc in range(LC):
                            ps = [S0[:, 0:512], S0[:, 512:1024],
                                  S1[:, 0:512], S1[:, 512:1024]][lc]
                            v.wait_ge(sMM, cnt["mm_ckv_%d_%d" % (ge, lc)])
                            i = v.tensor_copy(ckvsts[lc][:], ps)
                            cnt["ve_ckv_%d_%d" % (ge, lc)] = inc(i, sVE, 1)
                        tb = [ckg0, ckg1][ge % 2]
                        v.wait_ge(sMM, cnt["mm_krt_%d" % ge])
                        v.wait_ge(sTB[ge % 2], 16 * (ge // 2 + 1))
                        ilk(v, v.tensor_tensor(rtmpa[:], C0[0:PD, 0:512],
                                               tb[:, 0:512], ALU.mult), sVE)
                        ilk(v, v.tensor_tensor(rtmpb[:], C1[0:PD, 0:512],
                                               tb[:, 512:1024], ALU.mult), sVE)
                        i = v.tensor_add(krt_sb[:, ge * 512:(ge + 1) * 512], rtmpa[:], rtmpb[:])
                        cnt["rope_%d" % ge] = inc(i, sVE, 1)

                    @block.gpsimd
                    def _(sy, ge=ge):
                        for lc in range(LC):
                            sy.wait_ge(sVE, cnt["ve_ckv_%d_%d" % (ge, lc)])
                            i = sy.dma_start(ckv_sc[ge, :, lc * 512:(lc + 1) * 512],
                                             ckvsts[lc][:])
                            ckw_stage[lc] = inc(i, sCKW, 16)

                @block.gpsimd
                def _(gq, g=g):
                    tb = [ckg0, ckg1][g % 2]
                    if g >= 2:
                        gq.wait_ge(sVE, cnt["rope_%d" % (g - 2)])
                    i = gq.dma_start(tb[:].rearrange("p (u t) -> p u t", t=512),
                                     tabk_d[:, :, g * 512:(g + 1) * 512].rearrange("u p t -> p u t"))
                    cnt["tab_%d" % g] = inc(i, sTB[g % 2], 16)

                if g >= 1:
                    emit_evac(g - 1)

                @block.tensor
                def _(t_, g=g, rhs_of=rhs_of):
                    t_.wait_ge(sXT[g % 2], 16 * FC * (g // 2 + 1))
                    if g == 0:
                        t_.wait_ge(sW, 32)
                        t_.wait_ge(sVE, wkr_rot_done)
                    if g >= 2:
                        t_.wait_ge(sVE, cnt["rope_%d" % (g - 2)])
                    for lc in range(LC):
                        ps = [S0[:, 0:512], S0[:, 512:1024],
                              S1[:, 0:512], S1[:, 512:1024]][lc]
                        for f in range(FC):
                            i = t_.matmul(ps, wdkv_sb[:, (f * LC + lc) * 128:(f * LC + lc + 1) * 128],
                                          rhs_of(f), start=(f == 0), stop=(f == FC - 1))
                        cnt["mm_ckv_%d_%d" % (g, lc)] = inc(i, sMM, 1)
                    for f in range(FC):
                        i = t_.matmul(C0[0:PD, 0:512], wkr_sb[:, f * PD:(f + 1) * PD],
                                      rhs_of(f), start=(f == 0), stop=(f == FC - 1))
                    inc(i, sMM, 1)
                    for f in range(FC):
                        i = t_.matmul(C1[0:PD, 0:512], wkr_rot[:, f * PD:(f + 1) * PD],
                                      rhs_of(f), start=(f == 0), stop=(f == FC - 1))
                    mm_grp_done[g] = inc(i, sMM, 1)
                    cnt["mm_krt_%d" % g] = tot(sMM)

                if g == NG - 1:
                    emit_evac(g)

            ph0_mm = tot(sMM)
            ph0_ckw = tot(sCKW)
            ph0_sac = tot(sAC)
            ph0_sve = tot(sVE)
            s0_half_guard[0] = cnt["rope_%d" % (NG - 1)]
            s0_half_guard[1] = cnt["rope_%d" % (NG - 1)]
        _p0.close()

        if "krt" in dbg:
            @block.sync
            def _(sy):
                sy.wait_ge(sVE, ph0_sve)
                i = sy.dma_start(krt_dbg, krt_sb[:])
                inc(i, sOUT, 16)

        if stage == "p0":
            @block.gpsimd
            def _(g):
                g.wait_ge(sCKW, ph0_ckw)
                g.wait_ge(sXNT, tot(sXNT))
                i = g.dma_start(out_d[0:128, :], xnT_own[:, 0:2048])
                inc(i, sOUT, 16)
                g.wait_ge(sOUT, tot(sOUT))
            return nc

        _pa = contextlib.ExitStack()
        _st.enter_context(_pa)
        if True:
            EA = _pa.enter_context
            cqt_sb = EA(nc.sbuf_tensor("cqt_sb", [128, LC * OWN], BF16))
            cosq_sb = EA(nc.sbuf_tensor("cosq_sb", [PD, OWN], F32))
            sinq_sb = EA(nc.sbuf_tensor("sinq_sb", [PD, OWN], F32))
            wdqc0 = EA(nc.sbuf_tensor("wdqc0", [128, FC * 128], BF16))
            wdqc1 = EA(nc.sbuf_tensor("wdqc1", [128, FC * 128], BF16))
            wuk_sb = EA(nc.sbuf_tensor("wuk_sb", [128, LC * DM], BF16))
            wuv_sb = EA(nc.sbuf_tensor("wuv_sb", [128, LC * DM], BF16))
            wuq_sb = EA(nc.sbuf_tensor("wuq_sb", [128, LC * DM], BF16))
            wqr_sb = EA(nc.sbuf_tensor("wqr_sb", [128, LC * H * PD], BF16))
            wqr_rot = EA(nc.sbuf_tensor("wqr_rot", [128, LC * H * PD], BF16))
            kct_sb = EA(nc.sbuf_tensor("kct_sb", [128, TOK], BF16))
            vext_sb = EA(nc.sbuf_tensor("vext_sb", [128, 64 * 129], BF16))
            qct_h = EA(nc.sbuf_tensor("qct_h", [128, OWN], BF16))
            qrt_h = EA(nc.sbuf_tensor("qrt_h", [PD, OWN], BF16))
            qtmp_a = EA(nc.sbuf_tensor("qtmp_a", [PD, OWN], F32))
            qtmp_b = EA(nc.sbuf_tensor("qtmp_b", [PD, OWN], F32))
            expT0 = EA(nc.sbuf_tensor("expT0", [128, 1024], BF16))
            expT1 = EA(nc.sbuf_tensor("expT1", [128, 1024], BF16))
            ckvts = [EA(nc.sbuf_tensor("ckvt%d" % i, [128, LC * 512], BF16)) for i in range(5)]
            obuf = EA(nc.sbuf_tensor("obuf", [128, 8 * HD], BF16))
            ctx0_st = EA(nc.sbuf_tensor("ctx0_st", [128, 512], BF16)) if "ctx0" in dbg else None
            orcp = EA(nc.sbuf_tensor("orcp", [128, 1], F32))
            stab_sb = EA(nc.sbuf_tensor("stab_sb", [128, 1], F32))
            expTs = [expT0, expT1]

            @block.gpsimd
            def _(g):
                g.wait_ge(sMM, ph0_mm)
                g.wait_ge(sAC, ph0_sac)
                g.wait_ge(sVE, ph0_sve)
                g.wait_ge(sCKW, ph0_ckw)
                inc(g.dma_start(wuk_sb[:], wuk_d), sW, 16)
                inc(g.dma_start(wuv_sb[:], wuv_d), sW, 16)
                inc(g.dma_start(wuq_sb[:], wuq_d), sW, 16)
                inc(g.dma_start(cosq_sb[:], cosq_d), sW, 16)
                inc(g.dma_start(sinq_sb[:], sinq_d), sW, 16)
                cnt["w_att"] = inc(g.dma_start(wqr_sb[:], wqr_d), sW, 16)
                ones_view = vext_sb[:].rearrange("p (j c) -> p j c", c=129)
                g.memset(ones_view[:, :, 128:129], 1.0)
                i = g.memset(stab_sb[:], STAB)
                cnt["att_memset"] = inc(i, sW, 1)

            @block.vector
            def _(v):
                v.wait_ge(sW, cnt["w_att"])
                wv = wqr_sb[:].rearrange("p (c d) -> p c d", d=32)
                rv = wqr_rot[:].rearrange("p (c d) -> p c d", d=32)
                v.tensor_scalar(rv[:, 0::2, :], wv[:, 1::2, :], -1.0, None, ALU.mult)
                i = v.tensor_scalar(rv[:, 1::2, :], wv[:, 0::2, :], 1.0, None, ALU.mult)
                cnt["wqr_rot"] = inc(i, sVE, 1)

            wdqcs = [wdqc0, wdqc1]
            for lc in range(LC):
                @block.gpsimd
                def _(g, lc=lc):
                    if lc >= 2:
                        g.wait_ge(sMM, cnt["mm_cq_%d_%d" % (lc - 2, 1)])
                    i = g.dma_start(wdqcs[lc % 2][:],
                                    wdq_d[:, lc * FC * 128:(lc + 1) * FC * 128])
                    cnt["wdq_ld_%d" % lc] = inc(i, sGU[lc % 2], 16)

                for hf in range(2):
                    @block.tensor
                    def _(t_, lc=lc, hf=hf):
                        if hf == 0:
                            t_.wait_ge(sGU[lc % 2], cnt["wdq_ld_%d" % lc])
                        if lc == 0 and hf == 0:
                            t_.wait_ge(sW, cnt["w_att"])
                        if s0_half_guard[hf]:
                            t_.wait_ge(sVE, s0_half_guard[hf])
                        ps = S0[:, hf * 512:(hf + 1) * 512]
                        for f in range(FC):
                            i = t_.matmul(ps, wdqcs[lc % 2][:, f * 128:(f + 1) * 128],
                                          xnT_own[:, f * OWN + hf * 512: f * OWN + (hf + 1) * 512],
                                          start=(f == 0), stop=(f == FC - 1))
                        cnt["mm_cq_%d_%d" % (lc, hf)] = inc(i, sMM, 1)

                    @block.vector
                    def _(v, lc=lc, hf=hf):
                        v.wait_ge(sMM, cnt["mm_cq_%d_%d" % (lc, hf)])
                        i = v.tensor_copy(cqt_sb[:, lc * OWN + hf * 512: lc * OWN + (hf + 1) * 512],
                                          S0[:, hf * 512:(hf + 1) * 512])
                        s0_half_guard[hf] = inc(i, sVE, 1)
            cnt["cqt_done"] = tot(sVE)

            if "cqt" in dbg:
                @block.sync
                def _(sy):
                    sy.wait_ge(sVE, cnt["cqt_done"])
                    i = sy.dma_start(cqt_dbg, cqt_sb[:])
                    inc(i, sOUT, 16)

            A0 = tot(sAC)
            exp_done = {}
            for hh in range(nheads):
                for jj in range(64):
                    exp_done[(hh, jj)] = A0 + hh * 64 + jj + 1
            kv = {}
            o_evac_done = 0
            last_exp = 0

            def ckv_fetch(g, h, c):
                buf = ckvts[c % 5]
                prev = kv.get(("mm", h, c - 5), kv.get(("mm", h - 1, 11 + c)))
                if prev:
                    g.wait_ge(sMM, prev)
                if h == 0 and c == 0:
                    g.wait_ge(sCKW, ph0_ckw)
                i = g.dma_start(buf[:], ckv_sc[c])
                kv[("rd", h, c)] = inc(i, sCK[c % 5], 16)

            pc_jobs = []
            if precast:
                for e in range(E + NSUB):
                    for c in range(ICE):
                        pc_jobs.append((gu_bf[e, c], gu_d[e, c]))
                    for lo in range(0, ICE, 2):
                        hi = min(ICE, lo + 2)
                        pc_jobs.append((dn_bf[e, lo:hi], dn_d[e, lo:hi]))
                for q in range(8):
                    pc_jobs.append((wo_bf[:, q * 4096:(q + 1) * 4096],
                                    wo_d[:, q * 4096:(q + 1) * 4096]))
            cnt["pc_total"] = 16 * len(pc_jobs)
            cnt["pc_wo"] = 16 * len(pc_jobs)
            pc_next = [0]
            pc_slots = [0]
            pc_nslots = max(1, (H - 1) * 13)

            def emit_pc(g):
                pc_slots[0] += 1
                want = len(pc_jobs) * pc_slots[0] // pc_nslots
                while pc_next[0] < min(want, len(pc_jobs)):
                    dst, src_ = pc_jobs[pc_next[0]]
                    inc(g.dma_start(dst, src_), sPC, 16)
                    pc_next[0] += 1

            for h in range(nheads):
                @block.gpsimd
                def _(g, h=h):
                    for c in range(3):
                        ckv_fetch(g, h, c)

                @block.tensor
                def _(t_, h=h):
                    if h == 0:
                        t_.wait_ge(sVE, cnt["cqt_done"])
                        t_.wait_ge(sVE, cnt["wqr_rot"])
                    else:
                        t_.wait_ge(sAC, last_exp)
                        t_.wait_ge(sVE, o_evac_done)
                    for hf in range(2):
                        ps = S1[:, hf * 512:(hf + 1) * 512]
                        for lc in range(LC):
                            i = t_.matmul(ps, wuq_sb[:, lc * DM + h * HD: lc * DM + (h + 1) * HD],
                                          cqt_sb[:, lc * OWN + hf * 512: lc * OWN + (hf + 1) * 512],
                                          start=(lc == 0), stop=(lc == LC - 1))
                    cnt["mm_qc_%d" % h] = inc(i, sMM, 1)
                    for hf in range(2):
                        ps = S0[0:PD, hf * 512:(hf + 1) * 512]
                        for lc in range(LC):
                            i = t_.matmul(ps, wqr_sb[:, lc * H * PD + h * PD: lc * H * PD + (h + 1) * PD],
                                          cqt_sb[:, lc * OWN + hf * 512: lc * OWN + (hf + 1) * 512],
                                          start=(lc == 0), stop=(lc == LC - 1))
                    for hf in range(2):
                        ps = CT[2 + hf][0:PD, 0:512]
                        for lc in range(LC):
                            i = t_.matmul(ps, wqr_rot[:, lc * H * PD + h * PD: lc * H * PD + (h + 1) * PD],
                                          cqt_sb[:, lc * OWN + hf * 512: lc * OWN + (hf + 1) * 512],
                                          start=(lc == 0), stop=(lc == LC - 1))
                    cnt["mm_qr_%d" % h] = inc(i, sMM, 1)

                @block.vector
                def _(v, h=h):
                    v.wait_ge(sMM, cnt["mm_qc_%d" % h])
                    v.tensor_copy(qct_h[:, 0:512], S1[:, 0:512])
                    i = v.tensor_copy(qct_h[:, 512:1024], S1[:, 512:1024])
                    inc(i, sVE, 1)
                    v.wait_ge(sMM, cnt["mm_qr_%d" % h])
                    v.tensor_tensor(qtmp_a[:, 0:512], S0[0:PD, 0:512], cosq_sb[:, 0:512], ALU.mult)
                    v.tensor_tensor(qtmp_a[:, 512:1024], S0[0:PD, 512:1024], cosq_sb[:, 512:1024], ALU.mult)
                    v.tensor_tensor(qtmp_b[:, 0:512], CT[2][0:PD, 0:512], sinq_sb[:, 0:512], ALU.mult)
                    ilk(v, v.tensor_tensor(qtmp_b[:, 512:1024], CT[3][0:PD, 0:512], sinq_sb[:, 512:1024], ALU.mult), sVE)
                    i = v.tensor_add(qrt_h[:], qtmp_a[:], qtmp_b[:])
                    cnt["q_ready_%d" % h] = inc(i, sVE, 1)

                for c in range(16):
                    @block.tensor
                    def _(t_, h=h, c=c):
                        buf = ckvts[c % 5]
                        t_.wait_ge(sCK[c % 5], kv[("rd", h, c)])
                        if c == 0:
                            t_.wait_ge(sVE, cnt["q_ready_%d" % h])
                        half = c % 2
                        prevk = kv.get(("kev", h, c - 2), kv.get(("kev", h - 1, 14 + c)))
                        if prevk:
                            t_.wait_ge(sVE, prevk)
                        ps_k = S0[:, half * 512:(half + 1) * 512]
                        for lc in range(LC):
                            i = t_.matmul(ps_k, wuk_sb[:, lc * DM + h * HD: lc * DM + (h + 1) * HD],
                                          buf[:, lc * 512:(lc + 1) * 512],
                                          start=(lc == 0), stop=(lc == LC - 1))
                        kv[("kmm", h, c)] = inc(i, sMM, 1)
                        prevv = kv.get(("vev", h, c - 2), kv.get(("vev", h - 1, 14 + c)))
                        if prevv:
                            t_.wait_ge(sVE, prevv)
                        ps_v = S1[:, half * 512:(half + 1) * 512]
                        for sub in range(4):
                            pv = ps_v[:, sub * 128:(sub + 1) * 128]
                            for lc in range(LC):
                                i = t_.matmul(pv, buf[:, lc * 512 + sub * 128: lc * 512 + (sub + 1) * 128],
                                              wuv_sb[:, lc * DM + h * HD: lc * DM + (h + 1) * HD],
                                              start=(sub == 0 and lc == 0),
                                              stop=(sub == 3 and lc == LC - 1))
                        kv[("mm", h, c)] = inc(i, sMM, 1)

                    if c + 3 < 16:
                        @block.gpsimd
                        def _(g, h=h, c=c):
                            ckv_fetch(g, h, c + 3)
                            if h >= 1:
                                emit_pc(g)

                    @block.vector
                    def _(v, h=h, c=c):
                        half = c % 2
                        v.wait_ge(sMM, kv[("kmm", h, c)])
                        i = v.tensor_copy(kct_sb[:, c * 512:(c + 1) * 512],
                                          S0[:, half * 512:(half + 1) * 512])
                        kv[("kev", h, c)] = inc(i, sVE, 1)
                        v.wait_ge(sMM, kv[("mm", h, c)])
                        vv = vext_sb[:].rearrange("p (j c2) -> p j c2", c2=129)
                        i = v.tensor_copy(vv[:, 4 * c:4 * c + 4, 0:128],
                                          S1[:, half * 512:(half + 1) * 512].rearrange("p (s d) -> p s d", d=128))
                        kv[("vev", h, c)] = inc(i, sVE, 1)

                @block.tensor
                def _(t_, h=h):
                    def scores(j):
                        if j < 2:
                            t_.wait_ge(sVE, kv[("kev", h, 15)])
                            t_.wait_ge(sVE, kv[("vev", h, 15)])
                            prev_exp = exp_done.get((h - 1, 62 + j))
                            if prev_exp:
                                t_.wait_ge(sAC, prev_exp)
                        Sb = SS[j % 2]
                        for hf in range(2):
                            ps = Sb[:, hf * 512:(hf + 1) * 512]
                            t_.matmul(ps, kct_sb[:, j * 128:(j + 1) * 128],
                                      qct_h[:, hf * 512:(hf + 1) * 512], start=True, stop=False)
                            i = t_.matmul(ps, krt_sb[:, j * 128:(j + 1) * 128],
                                          qrt_h[:, hf * 512:(hf + 1) * 512], start=False, stop=True)
                        cnt["sc_%d_%d" % (h, j)] = inc(i, sMM, 1)

                    if h == 0:
                        t_.wait_ge(sW, cnt["att_memset"])
                    scores(0)
                    scores(1)
                    for j in range(64):
                        t_.wait_ge(sAC, exp_done[(h, j)])
                        if j == 0 and o_evac_done:
                            t_.wait_ge(sVE, o_evac_done)
                        for qt in range(8):
                            off = (qt % 2) * 256
                            i = t_.matmul(CT[qt // 2][:, off:off + 129],
                                          expTs[j % 2][:, qt * 128:(qt + 1) * 128],
                                          vext_sb[:, j * 129:(j + 1) * 129],
                                          start=(j == 0 and qt % 2 == 0),
                                          stop=(j == 63 and qt % 2 == 1))
                        cnt["ctx_%d_%d" % (h, j)] = inc(i, sMM, 1)
                        if j + 2 < 64:
                            scores(j + 2)

                last_exp = exp_done[(h, 63)]

                if h == 0 and "ctx0" in dbg:
                    @block.vector
                    def _(v):
                        v.wait_ge(sMM, cnt["ctx_0_63"])
                        i = v.tensor_copy(ctx0_st[:], C0[:, 0:512])
                        cnt["ctx0_ev"] = inc(i, sVE, 1)

                    @block.sync
                    def _(sy):
                        sy.wait_ge(sVE, cnt["ctx0_ev"])
                        inc(sy.dma_start(ctx0_dbg, ctx0_st[:]), sOUT, 16)

                @block.vector
                def _(v, h=h):
                    v.wait_ge(sMM, cnt["ctx_%d_%d" % (h, 63)])
                    if h > 0:
                        v.wait_ge(sOS, cnt["osw_%d" % (h - 1)])
                    for qt in range(8):
                        off = (qt % 2) * 256
                        ilk(v, v.reciprocal(orcp[:], CT[qt // 2][:, off + 128:off + 129]), sVE)
                        i = v.tensor_scalar(obuf[:, qt * HD:(qt + 1) * HD],
                                            CT[qt // 2][:, off:off + 128],
                                            orcp[:], None, ALU.mult)
                        cnt["oev_%d" % (h * 8 + qt)] = inc(i, sVE, 1)

                @block.sync
                def _(sy, h=h):
                    sy.wait_ge(sVE, cnt["oev_%d" % (h * 8 + 7)])
                    for qt in range(8):
                        i = sy.dma_start(o_sc[qt * 128:(qt + 1) * 128, h * HD:(h + 1) * HD],
                                         obuf[:, qt * HD:(qt + 1) * HD])
                        cnt["osw_%d" % h] = inc(i, sOS, 16)
                o_evac_done = cnt["oev_%d" % (h * 8 + 7)]

            @block.scalar
            def _(s):
                s.wait_ge(sW, cnt["att_memset"])
                for h in range(nheads):
                    for j in range(64):
                        s.wait_ge(sMM, cnt["sc_%d_%d" % (h, j)])
                        pc = cnt.get("ctx_%d_%d" % (h, j - 2),
                                     cnt.get("ctx_%d_%d" % (h - 1, 62 + j)))
                        if pc:
                            s.wait_ge(sMM, pc)
                        i = s.activation(expTs[j % 2][:], SS[j % 2][:], AF.Exp,
                                         bias=stab_sb[:], scale=SCALE)
                        got = inc(i, sAC, 1)
                        assert got == exp_done[(h, j)], (got, exp_done[(h, j)])

            if pc_jobs and pc_next[0] < len(pc_jobs):
                @block.gpsimd
                def _(g):
                    while pc_next[0] < len(pc_jobs):
                        dst, src_ = pc_jobs[pc_next[0]]
                        inc(g.dma_start(dst, src_), sPC, 16)
                        pc_next[0] += 1

            att_os = tot(sOS)
            att_mm = tot(sMM)
            att_sac = tot(sAC)
            att_sve = tot(sVE)

            if "kvq" in dbg:
                @block.sync
                def _(sy):
                    sy.wait_ge(sMM, att_mm)
                    sy.wait_ge(sAC, att_sac)
                    inc(sy.dma_start(kct_dbg, kct_sb[:]), sOUT, 16)
                    inc(sy.dma_start(vext_dbg, vext_sb[:]), sOUT, 16)
                    inc(sy.dma_start(qct_dbg, qct_h[:]), sOUT, 16)
                    inc(sy.dma_start(qrt_dbg, qrt_h[:]), sOUT, 16)
                    inc(sy.dma_start(expt_dbg, expTs[1][:]), sOUT, 16)
        _pa.close()
        _pk.close()

        if stage == "attn":
            @block.gpsimd
            def _(g):
                g.wait_ge(sOS, att_os)
                i = g.dma_start(out_d[0:128, :], xnT_own[:, 0:2048])
                inc(i, sOUT, 16)
                g.wait_ge(sOUT, tot(sOUT))
            return nc

        _pw = contextlib.ExitStack()
        _st.enter_context(_pw)
        if True:
            EW = _pw.enter_context
            out_acc = EW(nc.sbuf_tensor("out_acc", [128, 8 * DM], F32))
            _pw2 = contextlib.ExitStack()
            _pw.enter_context(_pw2)
            wo_sb = _pw2.enter_context(nc.sbuf_tensor("wo_sb", [128, FC * DM], BF16))
            oT_sb = _pw2.enter_context(nc.sbuf_tensor("oT_sb", [128, FC * OWN], BF16))
            xr0 = _pw2.enter_context(nc.sbuf_tensor("xr0", [128, DM], F32))

            @block.gpsimd
            def _(g):
                g.wait_ge(sMM, att_mm)
                g.wait_ge(sAC, att_sac)
                g.wait_ge(sVE, att_sve)
                if cnt["pc_wo"]:
                    g.wait_ge(sPC, cnt["pc_wo"])
                    i = g.dma_start(wo_sb[:], wo_bf)
                    cnt["wo_ld"] = inc(i, sW, 16)
                else:
                    for f in range(FC):
                        i = g.dma_start(wo_sb[:, f * DM:(f + 1) * DM],
                                        wo_d[:, f * DM:(f + 1) * DM])
                        cnt["wo_ld"] = inc(i, sW, 16)

            @block.scalar
            def _(s):
                s.wait_ge(sOS, att_os)
                s.wait_ge(sMM, att_mm)
                for f in range(FC):
                    i = s.dma_start_transpose(oT_sb[:, f * OWN:(f + 1) * OWN],
                                              o_sc[:, f * 128:(f + 1) * 128])
                    cnt["oT_ld"] = inc(i, sOT, 16)

            if "oacc" in dbg:
                @block.sync
                def _(sy):
                    sy.wait_ge(sOT, cnt["oT_ld"])
                    inc(sy.dma_start(ot2_dbg, oT_sb[:]), sOUT, 16)

            wo_mm = {}
            for tt in range(8):
                @block.sync
                def _(sy, tt=tt):
                    if tt >= 1:
                        sy.wait_ge(sVE, cnt["woev_%d_%d" % (tt - 1, 3)])
                    i = sy.dma_start(xr0[:], x_d[tt * 128:(tt + 1) * 128, :])
                    cnt["xr_%d" % tt] = inc(i, sX, 16)

                for nn in range(4):
                    @block.tensor
                    def _(t_, tt=tt, nn=nn):
                        if tt == 0 and nn == 0:
                            t_.wait_ge(sW, cnt["wo_ld"])
                            t_.wait_ge(sOT, cnt["oT_ld"])
                            t_.wait_ge(sAC, att_sac)
                        half = nn % 2
                        prev = cnt.get("woev_%d_%d" % (tt, nn - 2),
                                       cnt.get("woev_%d_%d" % (tt - 1, nn + 2)))
                        if prev:
                            t_.wait_ge(sVE, prev)
                        ps = S0[:, half * 512:(half + 1) * 512]
                        for f in range(FC):
                            i = t_.matmul(ps, oT_sb[:, f * OWN + tt * 128: f * OWN + (tt + 1) * 128],
                                          wo_sb[:, f * DM + nn * 512: f * DM + (nn + 1) * 512],
                                          start=(f == 0), stop=(f == FC - 1))
                        wo_mm[(tt, nn)] = inc(i, sMM, 1)

                    @block.vector
                    def _(v, tt=tt, nn=nn):
                        if nn == 0:
                            v.wait_ge(sX, cnt["xr_%d" % tt])
                        half = nn % 2
                        v.wait_ge(sMM, wo_mm[(tt, nn)])
                        i = v.tensor_add(out_acc[:, tt * DM + nn * 512: tt * DM + (nn + 1) * 512],
                                         S0[:, half * 512:(half + 1) * 512],
                                         xr0[:, nn * 512:(nn + 1) * 512])
                        cnt["woev_%d_%d" % (tt, nn)] = inc(i, sVE, 1)
            cnt["wo_done"] = tot(sVE)

            if "oacc" in dbg:
                @block.sync
                def _(sy):
                    sy.wait_ge(sVE, cnt["wo_done"])
                    inc(sy.dma_start(oacc_dbg, out_acc[:]), sOUT, 16)
                    inc(sy.dma_start(ot_dbg, oT_sb[:]), sOUT, 16)
                    inc(sy.dma_start(wosb_dbg, wo_sb[:]), sOUT, 16)
                    inc(sy.dma_start(xr_dbg, xr0[:]), sOUT, 16)

            if stage == "wo":
                @block.sync
                def _(sy):
                    sy.wait_ge(sVE, cnt["wo_done"])
                    for tt in range(8):
                        i = sy.dma_start(out_d[tt * 128:(tt + 1) * 128, :],
                                         out_acc[:, tt * DM:(tt + 1) * DM])
                        inc(i, sOUT, 16)
                    sy.wait_ge(sOUT, tot(sOUT))
                return nc

            _pw2.close()
            EF = _pw.enter_context
            rtr_sb = EF(nc.sbuf_tensor("rtr_sb", [128, FC * E], BF16))
            sgw_sb = EF(nc.sbuf_tensor("sgw_sb", [128, FC], BF16))
            logit_sb = EF(nc.sbuf_tensor("logit_sb", [128, 8 * E], F32))
            comb_sb = EF(nc.sbuf_tensor("comb_sb", [128, 8 * E], F32))
            sg_sb = EF(nc.sbuf_tensor("sg_sb", [128, 8], F32))
            diff_sb = EF(nc.sbuf_tensor("diff_sb", [128, 8], F32))
            c1_sb = EF(nc.sbuf_tensor("c1_sb", [128, 8], F32))
            m1_sb = EF(nc.sbuf_tensor("m1_sb", [128, 1], F32))
            m2_sb = EF(nc.sbuf_tensor("m2_sb", [128, 1], F32))
            msk1 = EF(nc.sbuf_tensor("msk1", [128, E], F32))
            msk2 = EF(nc.sbuf_tensor("msk2", [128, E], F32))
            lgt = EF(nc.sbuf_tensor("lgt", [128, E], F32))
            gub0 = EF(nc.sbuf_tensor("gub0", [128, 2 * DM], BF16))
            gub1 = EF(nc.sbuf_tensor("gub1", [128, 2 * DM], BF16))
            dnt_sb = EF(nc.sbuf_tensor("dnt_sb", [128, ICE * DM], BF16))
            hT_sb = EF(nc.sbuf_tensor("hT_sb", [128, ICE * OWN], BF16))
            silu_sb = EF(nc.sbuf_tensor("silu_sb", [128, 1024], F32))
            if True:
                gubs = [gub0, gub1]

                @block.gpsimd
                def _(g):
                    g.wait_ge(sMM, att_mm)
                    g.wait_ge(sAC, att_sac)
                    g.wait_ge(sVE, cnt["wo_done"])
                    inc(g.dma_start(rtr_sb[:], rtr_d), sW, 16)
                    cnt["rtr_ld"] = inc(g.dma_start(sgw_sb[:], sgw_d), sW, 16)

                for tt in range(8):
                    @block.tensor
                    def _(t_, tt=tt):
                        if tt == 0:
                            t_.wait_ge(sW, cnt["rtr_ld"])
                            t_.wait_ge(sVE, cnt["wo_done"])
                            t_.wait_ge(sAC, att_sac)
                        else:
                            t_.wait_ge(sVE, cnt["rt_ev_%d" % (tt - 1)])
                        ps = C0[:, 0:E]
                        for f in range(FC):
                            t_.matmul(ps, xnT_own[:, f * OWN + tt * 128: f * OWN + (tt + 1) * 128],
                                      rtr_sb[:, f * E:(f + 1) * E],
                                      start=(f == 0), stop=(f == FC - 1))
                        ps2 = C1[:, 0:1]
                        for f in range(FC):
                            i2 = t_.matmul(ps2, xnT_own[:, f * OWN + tt * 128: f * OWN + (tt + 1) * 128],
                                           sgw_sb[:, f:f + 1],
                                           start=(f == 0), stop=(f == FC - 1))
                        cnt["rt_mm_%d" % tt] = inc(i2, sMM, 1)

                    @block.vector
                    def _(v, tt=tt):
                        v.wait_ge(sMM, cnt["rt_mm_%d" % tt])
                        v.tensor_copy(logit_sb[:, tt * E:(tt + 1) * E], C0[:, 0:E])
                        i = v.tensor_copy(sg_sb[:, tt:tt + 1], C1[:, 0:1])
                        cnt["rt_ev_%d" % tt] = inc(i, sVE, 1)

                @block.vector
                def _(v):
                    for tt in range(8):
                        L = logit_sb[:, tt * E:(tt + 1) * E]
                        ilk(v, v.reduce_max(m1_sb[:], L, axis=X), sVE)
                        ilk(v, v.tensor_scalar(msk1[:], L, m1_sb[:], None, ALU.is_equal), sVE)
                        ilk(v, v.scalar_tensor_tensor(lgt[:], msk1[:], -1e30, L, ALU.mult, ALU.add), sVE)
                        ilk(v, v.reduce_max(m2_sb[:], lgt[:], axis=X), sVE)
                        ilk(v, v.tensor_scalar(msk2[:], lgt[:], m2_sb[:], None, ALU.is_equal), sVE)
                        ilk(v, v.tensor_sub(diff_sb[:, tt:tt + 1], m2_sb[:], m1_sb[:]), sVE)
                        v.tensor_copy(comb_sb[:, tt * E:(tt + 1) * E], msk1[:])
                        i = v.tensor_copy(logit_sb[:, tt * E:(tt + 1) * E], msk2[:])
                        cnt["rt_msk_%d" % tt] = inc(i, sVE, 1)

                @block.scalar
                def _(s):
                    s.wait_ge(sVE, cnt["rt_msk_7"])
                    i = s.activation(c1_sb[:], diff_sb[:], AF.Sigmoid)
                    inc(i, sAC, 1)
                    i = s.activation(sg_sb[:], sg_sb[:], AF.Sigmoid)
                    cnt["sig_done"] = inc(i, sAC, 1)

                @block.vector
                def _(v):
                    v.wait_ge(sAC, cnt["sig_done"])
                    for tt in range(8):
                        c2 = c1_sb[:, tt:tt + 1]
                        ilk(v, v.tensor_scalar(m1_sb[:], c2, -1.0, 1.0, ALU.mult, ALU.add), sVE)
                        ilk(v, v.tensor_scalar(msk1[:], comb_sb[:, tt * E:(tt + 1) * E], m1_sb[:], None, ALU.mult), sVE)
                        ilk(v, v.tensor_scalar(msk2[:], logit_sb[:, tt * E:(tt + 1) * E], c2, None, ALU.mult), sVE)
                        i = v.tensor_add(comb_sb[:, tt * E:(tt + 1) * E], msk1[:], msk2[:])
                        inc(i, sVE, 1)
                    cnt["comb_done"] = tot(sVE)

                if "comb" in dbg:
                    @block.sync
                    def _(sy):
                        sy.wait_ge(sVE, cnt["comb_done"])
                        i = sy.dma_start(comb_dbg, comb_sb[:])
                        inc(i, sOUT, 16)
                        i = sy.dma_start(sg_dbg, sg_sb[:])
                        inc(i, sOUT, 16)

                def expert(idx, gu_src, dn_src, comb_col):
                    @block.gpsimd
                    def _(g, idx=idx):
                        prev = cnt.get("dn_used_%d" % (idx - 1))
                        if prev:
                            g.wait_ge(sMM, prev)
                        if idx == 0 and cnt["pc_total"]:
                            g.wait_ge(sPC, cnt["pc_total"])
                        for half in range(2):
                            lo = half * 6
                            hi = min(ICE, lo + 6)
                            i = g.dma_start(
                                dnt_sb[:, lo * DM:hi * DM].rearrange("p (c f) -> p c f", f=DM),
                                dn_src(lo, hi).rearrange("c p f -> p c f"))
                            cnt["dn_ld_%d" % idx] = inc(i, sDN, 16)

                    for c in range(ICE):
                        slot = c % 2

                        @block.gpsimd
                        def _(g, idx=idx, c=c, slot=slot):
                            prev = cnt.get("gu_used_%d_%d" % (idx, c - 2),
                                           cnt.get("gu_used_%d_%d" % (idx - 1, ICE - 2 + c)))
                            if prev:
                                g.wait_ge(sMM, prev)
                            i = g.dma_start(
                                gubs[slot][:].rearrange("p (u f) -> p u f", f=DM),
                                gu_src(c).rearrange("u p f -> p u f"))
                            cnt["gu_ld_%d" % idx + "_%d" % c] = inc(i, sGU[slot], 16)

                        @block.tensor
                        def _(t_, idx=idx, c=c, slot=slot):
                            t_.wait_ge(sGU[slot], cnt["gu_ld_%d_%d" % (idx, c)])
                            for part, SP in ((0, S0), (1, S1)):
                                if part == 0:
                                    prev = cnt.get("sil_%d_%d" % (idx, c - 1),
                                                   cnt.get("sil_%d_%d" % (idx - 1, ICE - 1)))
                                    if prev:
                                        t_.wait_ge(sAC, prev)
                                else:
                                    prev = cnt.get("hev_%d_%d" % (idx, c - 1),
                                                   cnt.get("hev_%d_%d" % (idx - 1, ICE - 1)))
                                    if prev:
                                        t_.wait_ge(sVE, prev)
                                for hf in range(2):
                                    ps = SP[:, hf * 512:(hf + 1) * 512]
                                    for f in range(FC):
                                        i = t_.matmul(
                                            ps, gubs[slot][:, part * DM + f * 128: part * DM + (f + 1) * 128],
                                            xnT_own[:, f * OWN + hf * 512: f * OWN + (hf + 1) * 512],
                                            start=(f == 0), stop=(f == FC - 1))
                                inc(i, sMM, 1)
                            cnt["gu_used_%d_%d" % (idx, c)] = tot(sMM)
                            cnt["gu_mm_%d_%d" % (idx, c)] = tot(sMM)

                        @block.scalar
                        def _(s, idx=idx, c=c):
                            s.wait_ge(sMM, cnt["gu_mm_%d_%d" % (idx, c)])
                            prev = cnt.get("hmul_%d_%d_1" % (idx, c - 1),
                                           cnt.get("hmul_%d_%d_1" % (idx - 1, ICE - 1)))
                            if prev:
                                s.wait_ge(sVE, prev)
                            for hf in range(2):
                                i = s.activation(silu_sb[:, hf * 512:(hf + 1) * 512],
                                                 S0[:, hf * 512:(hf + 1) * 512], AF.Silu)
                                cnt["sil_%d_%d_%d" % (idx, c, hf)] = inc(i, sAC, 1)
                            cnt["sil_%d_%d" % (idx, c)] = tot(sAC)

                        @block.vector
                        def _(v, idx=idx, c=c):
                            for hf in range(2):
                                v.wait_ge(sAC, cnt["sil_%d_%d_%d" % (idx, c, hf)])
                                i = v.tensor_tensor(
                                    hT_sb[:, c * OWN + hf * 512: c * OWN + (hf + 1) * 512],
                                    silu_sb[:, hf * 512:(hf + 1) * 512],
                                    S1[:, hf * 512:(hf + 1) * 512], ALU.mult)
                                cnt["hmul_%d_%d_%d" % (idx, c, hf)] = inc(i, sVE, 1)
                            cnt["hev_%d_%d" % (idx, c)] = tot(sVE)

                    @block.tensor
                    def _(t_, idx=idx):
                        t_.wait_ge(sDN, cnt["dn_ld_%d" % idx])
                        t_.wait_ge(sVE, cnt["hev_%d_%d" % (idx, ICE - 1)])
                        for tt in range(8):
                            for nn in range(4):
                                k = tt * 4 + nn
                                ps = CT[k % 2][:, 0:512]
                                prev = cnt.get("dn_ev_%d_%d" % (idx, k - 2),
                                               cnt.get("dn_ev_%d_%d" % (idx - 1, 30 + (k % 2))))
                                if prev:
                                    t_.wait_ge(sVE, prev)
                                for ic in range(ICE):
                                    i = t_.matmul(ps, hT_sb[:, ic * OWN + tt * 128: ic * OWN + (tt + 1) * 128],
                                                  dnt_sb[:, ic * DM + nn * 512: ic * DM + (nn + 1) * 512],
                                                  start=(ic == 0), stop=(ic == ICE - 1))
                                cnt["dn_mm_%d_%d" % (idx, k)] = inc(i, sMM, 1)
                        cnt["dn_used_%d" % idx] = tot(sMM)

                    @block.vector
                    def _(v, idx=idx, comb_col=comb_col):
                        for tt in range(8):
                            for nn in range(4):
                                k = tt * 4 + nn
                                v.wait_ge(sMM, cnt["dn_mm_%d_%d" % (idx, k)])
                                i = v.scalar_tensor_tensor(
                                    out_acc[:, tt * DM + nn * 512: tt * DM + (nn + 1) * 512],
                                    CT[k % 2][:, 0:512], comb_col(tt),
                                    out_acc[:, tt * DM + nn * 512: tt * DM + (nn + 1) * 512],
                                    ALU.mult, ALU.add)
                                cnt["dn_ev_%d_%d" % (idx, k)] = inc(i, sVE, 1)
                        cnt["exp_done_%d" % idx] = tot(sVE)

                gu_w = gu_bf if cnt["pc_total"] else gu_d
                dn_w = dn_bf if cnt["pc_total"] else dn_d
                for e in range(nexp):
                    expert(e,
                           gu_src=lambda c, e=e: gu_w[e, c, :, :, :],
                           dn_src=lambda lo, hi, e=e: dn_w[e, lo:hi, :, :],
                           comb_col=(lambda tt, e=e: comb_sb[:, tt * E + e: tt * E + e + 1])
                                    if e < E else (lambda tt: sg_sb[:, tt:tt + 1]))

                if "ffn0" in dbg:
                    @block.sync
                    def _(sy):
                        sy.wait_ge(sVE, cnt["exp_done_%d" % (nexp - 1)] if nexp else cnt["comb_done"])
                        inc(sy.dma_start(ht_dbg, hT_sb[:]), sOUT, 16)
                        inc(sy.dma_start(gub_dbg[0], gub0[:]), sOUT, 16)
                        inc(sy.dma_start(gub_dbg[1], gub1[:]), sOUT, 16)
                        inc(sy.dma_start(dnt_dbg, dnt_sb[:]), sOUT, 16)
                        inc(sy.dma_start(silu_dbg, silu_sb[:]), sOUT, 16)

                @block.sync
                def _(sy):
                    sy.wait_ge(sVE, cnt["exp_done_%d" % (nexp - 1)] if nexp else cnt["comb_done"])
                    for tt in range(8):
                        i = sy.dma_start(out_d[tt * 128:(tt + 1) * 128, :],
                                         out_acc[:, tt * DM:(tt + 1) * DM])
                        inc(i, sOUT, 16)
                    sy.wait_ge(sOUT, tot(sOUT))
    return nc


def _rope_tables():
    inv_freq = 1.0 / 10000.0 ** (np.arange(0, PD, 2, dtype=np.float32) / PD)
    t = np.arange(TOK, dtype=np.float32)
    freqs = t[:, None] * inv_freq[None, :]
    emb = np.concatenate([freqs, freqs], -1)
    return np.cos(emb).astype(np.float32), np.sin(emb).astype(np.float32)


def prep_inputs(
    L0_attn_hidden_states, L0_attn_W_DKV, L0_attn_W_DQ, L0_attn_W_UK,
    L0_attn_W_UV, L0_attn_W_UQ, L0_attn_W_KR, L0_attn_W_QR, L0_attn_W_O,
    L0_ffn_router_weight, L0_ffn_gate_up_proj, L0_ffn_down_proj,
    L0_ffn_sh_gate_w, L0_ffn_sh_up_w, L0_ffn_sh_down_w, L0_ffn_shared_gate_w,
):
    f32 = lambda a: np.ascontiguousarray(np.asarray(a, dtype=np.float32))
    x = f32(L0_attn_hidden_states).reshape(TOK, DM)
    cos, sin = _rope_tables()

    wdkv = f32(L0_attn_W_DKV).reshape(FC, 128, LC, 128).transpose(1, 0, 2, 3).reshape(128, FC * LC * 128)
    wdq = f32(L0_attn_W_DQ).reshape(FC, 128, LC, 128).transpose(1, 2, 0, 3).reshape(128, FC * LC * 128)
    wkr = f32(L0_attn_W_KR).reshape(FC, 128, PD).transpose(1, 0, 2).reshape(128, FC * PD)
    wuk = f32(L0_attn_W_UK).reshape(LC, 128, DM).transpose(1, 0, 2).reshape(128, LC * DM)
    wuv = f32(L0_attn_W_UV).reshape(LC, 128, DM).transpose(1, 0, 2).reshape(128, LC * DM)
    wuq = f32(L0_attn_W_UQ).reshape(LC, 128, DM).transpose(1, 0, 2).reshape(128, LC * DM)
    wqr = f32(L0_attn_W_QR).reshape(LC, 128, H * PD).transpose(1, 0, 2).reshape(128, LC * H * PD)
    wo = f32(L0_attn_W_O).reshape(FC, 128, DM).transpose(1, 0, 2).reshape(128, FC * DM)
    rtr = f32(L0_ffn_router_weight).T.reshape(FC, 128, E).transpose(1, 0, 2).reshape(128, FC * E)
    sgw = f32(L0_ffn_shared_gate_w).reshape(FC, 128).T.copy()
    gu_e = f32(L0_ffn_gate_up_proj).reshape(E, 2 * ICE, 128, FC, 128).transpose(0, 1, 4, 3, 2).reshape(E, 2 * ICE, 128, DM)
    dn_e = f32(L0_ffn_down_proj).transpose(0, 2, 1).reshape(E, ICE, 128, DM)
    shg = f32(L0_ffn_sh_gate_w).reshape(NSUB * ICE, 128, FC, 128).transpose(0, 3, 2, 1).reshape(NSUB * ICE, 128, DM)
    shu = f32(L0_ffn_sh_up_w).reshape(NSUB * ICE, 128, FC, 128).transpose(0, 3, 2, 1).reshape(NSUB * ICE, 128, DM)
    shd = f32(L0_ffn_sh_down_w).T.reshape(NSUB * ICE, 128, DM).copy()
    gu = np.empty((E + NSUB, ICE, 2, 128, DM), np.float32)
    gu[:E, :, 0] = gu_e[:, :ICE]
    gu[:E, :, 1] = gu_e[:, ICE:]
    gu[E:, :, 0] = shg.reshape(NSUB, ICE, 128, DM)
    gu[E:, :, 1] = shu.reshape(NSUB, ICE, 128, DM)
    dn = np.concatenate([dn_e, shd.reshape(NSUB, ICE, 128, DM)], axis=0)

    shared = dict(wdkv=wdkv, wdq=wdq, wkr=wkr, wuk=wuk, wuv=wuv, wuq=wuq,
                  wqr=wqr, wo=wo, rtr=rtr, sgw=sgw, gu=gu, dn=dn)
    shared = {k: np.ascontiguousarray(v) for k, v in shared.items()}

    in_maps = []
    for c in range(NCORE):
        own0 = c * OWN
        m = dict(shared)
        m["x"] = np.ascontiguousarray(np.roll(x, -own0, axis=0))
        cosr = np.roll(cos, -own0, axis=0)
        sinr = np.roll(sin, -own0, axis=0)
        m["tabk"] = np.ascontiguousarray(np.stack([cosr.T, sinr.T], axis=0))
        m["cosq"] = np.ascontiguousarray(cosr[:OWN].T)
        m["sinq"] = np.ascontiguousarray(sinr[:OWN].T)
        in_maps.append(m)
    return in_maps


def kernel(**inputs):
    nc = build_program()
    in_maps = prep_inputs(**inputs)
    res = run_bass_kernel_spmd(nc, in_maps, core_ids=list(range(NCORE)))
    out = np.concatenate([res.results[c]["out"] for c in range(NCORE)], axis=0)
    return out.reshape(1, TOK, DM).astype(np.float32)


# version 16
# speedup vs baseline: 1.2170x; 1.2170x over previous
import contextlib
import math
import numpy as np
import concourse.bass as bass
import concourse.mybir as mybir
from concourse.bass_utils import run_bass_kernel_spmd

F32 = mybir.dt.float32
BF16 = mybir.dt.bfloat16
AF = mybir.ActivationFunctionType
ALU = mybir.AluOpType

TOK = 8192
DM = 2048
OWN = 1024
NCORE = 8
LAT = 512
H = 16
HD = 128
PD = 64
FC = DM // 128
LC = LAT // 128
NG = TOK // 512
OG = OWN // 512
E = 8
IM = 1408
ICE = IM // 128
NSUB = 4
SCALE = 1.0 / math.sqrt(HD + PD)
STAB = -4.0
EPS = 1e-5
X = mybir.AxisListType.X


def build_program(stage="full", debug=(), nheads=H, nexp=E + NSUB, precast=False):
    nc = bass.Bass("TRN2", target_bir_lowering=False)
    dbg = set(debug)

    x_d = nc.dram_tensor("x", [TOK, DM], F32, kind="ExternalInput").ap()
    tabk_d = nc.dram_tensor("tabk", [2, PD, TOK], F32, kind="ExternalInput").ap()
    cosq_d = nc.dram_tensor("cosq", [PD, OWN], F32, kind="ExternalInput").ap()
    sinq_d = nc.dram_tensor("sinq", [PD, OWN], F32, kind="ExternalInput").ap()
    wdkv_d = nc.dram_tensor("wdkv", [128, FC * LC * 128], F32, kind="ExternalInput").ap()
    wdq_d = nc.dram_tensor("wdq", [128, FC * LC * 128], F32, kind="ExternalInput").ap()
    wkr_d = nc.dram_tensor("wkr", [128, FC * PD], F32, kind="ExternalInput").ap()
    wuk_d = nc.dram_tensor("wuk", [128, LC * DM], F32, kind="ExternalInput").ap()
    wuv_d = nc.dram_tensor("wuv", [128, LC * DM], F32, kind="ExternalInput").ap()
    wuq_d = nc.dram_tensor("wuq", [128, LC * DM], F32, kind="ExternalInput").ap()
    wqr_d = nc.dram_tensor("wqr", [128, LC * H * PD], F32, kind="ExternalInput").ap()
    wo_d = nc.dram_tensor("wo", [128, FC * DM], F32, kind="ExternalInput").ap()
    rtr_d = nc.dram_tensor("rtr", [128, FC * E], F32, kind="ExternalInput").ap()
    sgw_d = nc.dram_tensor("sgw", [128, FC], F32, kind="ExternalInput").ap()
    gu_d = nc.dram_tensor("gu", [E + NSUB, ICE, 2, 128, DM], F32, kind="ExternalInput").ap()
    dn_d = nc.dram_tensor("dn", [E + NSUB, ICE, 128, DM], F32, kind="ExternalInput").ap()
    out_d = nc.dram_tensor("out", [OWN, DM], F32, kind="ExternalOutput").ap()

    gu_bf = nc.dram_tensor("gu_bf", [E + NSUB, ICE, 2, 128, DM], BF16,
                           kind="ExternalOutput" if "gubf" in dbg else "Internal").ap()
    dn_bf = nc.dram_tensor("dn_bf", [E + NSUB, ICE, 128, DM], BF16,
                           kind="ExternalOutput" if "gubf" in dbg else "Internal").ap()
    wo_bf = nc.dram_tensor("wo_bf", [128, FC * DM], BF16, kind="Internal").ap()
    xn_sc = nc.dram_tensor("xn_sc", [TOK, DM], BF16,
                           kind="ExternalOutput" if "xn" in dbg else "Internal").ap()
    ckv_sc = nc.dram_tensor("ckv_sc", [NG, 128, LC * 512], BF16,
                            kind="ExternalOutput" if "ckv" in dbg else "Internal").ap()
    o_sc = nc.dram_tensor("o_sc", [OWN, H * HD], BF16,
                          kind="ExternalOutput" if "osc" in dbg else "Internal").ap()
    if "krt" in dbg:
        krt_dbg = nc.dram_tensor("krt_dbg", [PD, TOK], BF16, kind="ExternalOutput").ap()
    if "cqt" in dbg:
        cqt_dbg = nc.dram_tensor("cqt_dbg", [128, LC * OWN], BF16, kind="ExternalOutput").ap()
    if "kvq" in dbg:
        kct_dbg = nc.dram_tensor("kct_dbg", [128, TOK], BF16, kind="ExternalOutput").ap()
        vext_dbg = nc.dram_tensor("vext_dbg", [128, 64 * 129], BF16, kind="ExternalOutput").ap()
        qct_dbg = nc.dram_tensor("qct_dbg", [128, OWN], BF16, kind="ExternalOutput").ap()
        qrt_dbg = nc.dram_tensor("qrt_dbg", [PD, OWN], BF16, kind="ExternalOutput").ap()
        expt_dbg = nc.dram_tensor("expt_dbg", [128, 1024], BF16, kind="ExternalOutput").ap()
    if "ctx0" in dbg:
        ctx0_dbg = nc.dram_tensor("ctx0_dbg", [128, 512], BF16, kind="ExternalOutput").ap()
    if "oacc" in dbg:
        oacc_dbg = nc.dram_tensor("oacc_dbg", [128, 8 * DM], F32, kind="ExternalOutput").ap()
        ot_dbg = nc.dram_tensor("ot_dbg", [128, FC * OWN], BF16, kind="ExternalOutput").ap()
        ot2_dbg = nc.dram_tensor("ot2_dbg", [128, FC * OWN], BF16, kind="ExternalOutput").ap()
        wosb_dbg = nc.dram_tensor("wosb_dbg", [128, FC * DM], BF16, kind="ExternalOutput").ap()
        xr_dbg = nc.dram_tensor("xr_dbg", [128, DM], F32, kind="ExternalOutput").ap()
    if "ffn0" in dbg:
        ht_dbg = nc.dram_tensor("ht_dbg", [128, ICE * OWN], BF16, kind="ExternalOutput").ap()
        gub_dbg = nc.dram_tensor("gub_dbg", [2, 128, 2 * DM], BF16, kind="ExternalOutput").ap()
        dnt_dbg = nc.dram_tensor("dnt_dbg", [128, ICE * DM], BF16, kind="ExternalOutput").ap()
        silu_dbg = nc.dram_tensor("silu_dbg", [128, 1024], F32, kind="ExternalOutput").ap()
    if "comb" in dbg:
        comb_dbg = nc.dram_tensor("comb_dbg", [128, 8 * E], F32, kind="ExternalOutput").ap()
        sg_dbg = nc.dram_tensor("sg_dbg", [128, 8], F32, kind="ExternalOutput").ap()

    cnt = {}

    def inc(ins, s, n):
        cnt[s.name] = cnt.get(s.name, 0) + n
        ins.then_inc(s, n)
        return cnt[s.name]

    def tot(s):
        return cnt.get(s.name, 0)

    def ilk(eng, ins, s):
        c = inc(ins, s, 1)
        eng.wait_ge(s, c)
        return c

    with contextlib.ExitStack() as _st:
        EC = _st.enter_context
        sX = EC(nc.semaphore("sX"))
        sXNW = EC(nc.semaphore("sXNW"))
        sXNT = EC(nc.semaphore("sXNT"))
        sCKW = EC(nc.semaphore("sCKW"))
        sCKR = EC(nc.semaphore("sCKR"))
        sW = EC(nc.semaphore("sW"))
        sOS = EC(nc.semaphore("sOS"))
        sOT = EC(nc.semaphore("sOT"))
        sOUT = EC(nc.semaphore("sOUT"))
        sPC = EC(nc.semaphore("sPC"))
        sXL = [EC(nc.semaphore("sXL%d" % i)) for i in range(8)]
        sTB = [EC(nc.semaphore("sTB%d" % i)) for i in range(2)]
        sXT = [EC(nc.semaphore("sXT%d" % i)) for i in range(2)]
        sCK = [EC(nc.semaphore("sCK%d" % i)) for i in range(5)]
        sGU = [EC(nc.semaphore("sGU%d" % i)) for i in range(2)]
        sDN = EC(nc.semaphore("sDN"))
        sMM = EC(nc.semaphore("sMM"))
        sVE = EC(nc.semaphore("sVE"))
        sAC = EC(nc.semaphore("sAC"))
        xnT_own = EC(nc.sbuf_tensor("xnT_own", [128, FC * OWN], BF16))
        _pk = contextlib.ExitStack()
        _st.enter_context(_pk)
        krt_sb = _pk.enter_context(nc.sbuf_tensor("krt_sb", [PD, TOK], BF16))
        S0 = EC(nc.psum_tensor("S0", [128, 1024], F32))
        S1 = EC(nc.psum_tensor("S1", [128, 1024], F32))
        C0 = EC(nc.psum_tensor("C0", [128, 512], F32))
        C1 = EC(nc.psum_tensor("C1", [128, 512], F32))
        C2 = EC(nc.psum_tensor("C2", [128, 512], F32))
        C3 = EC(nc.psum_tensor("C3", [128, 512], F32))
        block = EC(nc.Block())
        1
        CT = [C0, C1, C2, C3]
        SS = [S0, S1]
        s0_half_guard = [0, 0]

        _p0 = contextlib.ExitStack()
        _st.enter_context(_p0)
        if True:
            E0 = _p0.enter_context
            ckg0 = E0(nc.sbuf_tensor("ckg0", [PD, 1024], F32))
            ckg1 = E0(nc.sbuf_tensor("ckg1", [PD, 1024], F32))
            xbufs = [E0(nc.sbuf_tensor("xbuf%d" % i, [128, DM], F32)) for i in range(8)]
            xnb0 = E0(nc.sbuf_tensor("xnb0", [128, DM], BF16))
            xnb1 = E0(nc.sbuf_tensor("xnb1", [128, DM], BF16))
            xnb2 = E0(nc.sbuf_tensor("xnb2", [128, DM], BF16))
            xnb3 = E0(nc.sbuf_tensor("xnb3", [128, DM], BF16))
            trash_sb = E0(nc.sbuf_tensor("trash_sb", [128, DM], BF16))
            st_sum = E0(nc.sbuf_tensor("st_sum", [128, 8], F32))
            st_ssq = E0(nc.sbuf_tensor("st_ssq", [128, 8], F32))
            st_m = E0(nc.sbuf_tensor("st_m", [128, 8], F32))
            st_negm = E0(nc.sbuf_tensor("st_negm", [128, 8], F32))
            st_msq = E0(nc.sbuf_tensor("st_msq", [128, 8], F32))
            st_var = E0(nc.sbuf_tensor("st_var", [128, 8], F32))
            st_rstd = E0(nc.sbuf_tensor("st_rstd", [128, 8], F32))
            st_nmr = E0(nc.sbuf_tensor("st_nmr", [128, 8], F32))
            xnTg0 = E0(nc.sbuf_tensor("xnTg0", [128, FC * 512], BF16))
            xnTg1 = E0(nc.sbuf_tensor("xnTg1", [128, FC * 512], BF16))
            wdkv_sb = E0(nc.sbuf_tensor("wdkv_sb", [128, FC * LC * 128], BF16))
            wkr_sb = E0(nc.sbuf_tensor("wkr_sb", [128, FC * PD], BF16))
            wkr_rot = E0(nc.sbuf_tensor("wkr_rot", [128, FC * PD], BF16))
            ckvsts = [E0(nc.sbuf_tensor("ckvst%d" % i, [128, 512], BF16)) for i in range(4)]
            rtmpa = E0(nc.sbuf_tensor("rtmpa", [PD, 512], F32))
            rtmpb = E0(nc.sbuf_tensor("rtmpb", [PD, 512], F32))
            xnbufs = [xnb0, xnb1, xnb2, xnb3]
            xnTgs = [xnTg0, xnTg1]

            @block.gpsimd
            def _(g):
                inc(g.dma_start(wdkv_sb[:], wdkv_d), sW, 16)
                inc(g.dma_start(wkr_sb[:], wkr_d), sW, 16)

            @block.vector
            def _(v):
                v.wait_ge(sW, 32)
                wv = wkr_sb[:].rearrange("p (c d) -> p c d", d=PD)
                rv = wkr_rot[:].rearrange("p (c d) -> p c d", d=PD)
                v.tensor_scalar(rv[:, :, 0:32], wv[:, :, 32:64], -1.0, None, ALU.mult)
                i = v.tensor_scalar(rv[:, :, 32:64], wv[:, :, 0:32], 1.0, None, ALU.mult)
                inc(i, sVE, 1)
            wkr_rot_done = tot(sVE)

            aff_done = [0] * 64
            xnw_done = [0] * 64
            xnt_done = [0] * NG
            mm_grp_done = [0] * NG
            s1_guard = 0
            ckw_stage = {}

            for t in range(64):
                g = t // 4
                t4 = t % 4
                pp = (g % 2) * 4 + t4

                @block.sync
                def _(sy, t=t):
                    if t >= 8:
                        sy.wait_ge(sAC, aff_done[t - 8])
                    i = sy.dma_start(xbufs[t % 8][:], x_d[t * 128:(t + 1) * 128, :])
                    inc(i, sXL[t % 8], 16)

                @block.vector
                def _(v, t=t, pp=pp):
                    v.wait_ge(sXL[t % 8], 16 * (t // 8 + 1))
                    v.reduce_sum(st_sum[:, pp:pp + 1], xbufs[t % 8][:], axis=X)
                    i = v.scalar_tensor_tensor(trash_sb[:], xbufs[t % 8][:], 1.0,
                                               xbufs[t % 8][:], ALU.mult, ALU.mult,
                                               accum_out=st_ssq[:, pp:pp + 1])
                    inc(i, sVE, 1)

                if t % 4 != 3:
                    continue

                q0 = (g % 2) * 4
                P4 = slice(q0, q0 + 4)

                @block.vector
                def _(v, g=g, P4=P4):
                    ilk(v, v.tensor_scalar(st_m[:, P4], st_sum[:, P4], 1.0 / DM, None, ALU.mult), sVE)
                    ilk(v, v.tensor_scalar(st_negm[:, P4], st_m[:, P4], -1.0, None, ALU.mult), sVE)
                    ilk(v, v.tensor_tensor(st_msq[:, P4], st_m[:, P4], st_m[:, P4], ALU.mult), sVE)
                    ilk(v, v.tensor_scalar(st_var[:, P4], st_ssq[:, P4], 1.0 / DM, None, ALU.mult), sVE)
                    ilk(v, v.tensor_sub(st_var[:, P4], st_var[:, P4], st_msq[:, P4]), sVE)
                    ilk(v, v.tensor_scalar(st_var[:, P4], st_var[:, P4], EPS, None, ALU.add), sVE)
                    i = v.reciprocal(st_var[:, P4], st_var[:, P4])
                    cnt["ln_chain_%d" % g] = inc(i, sVE, 1)

                @block.scalar
                def _(s, g=g, P4=P4, q0=q0):
                    s.wait_ge(sVE, cnt["ln_chain_%d" % g])
                    i = s.activation(st_rstd[:, P4], st_var[:, P4], AF.Sqrt)
                    inc(i, sAC, 1)
                    s.wait_ge(sAC, tot(sAC))
                    for t4 in range(4):
                        t = 4 * g + t4
                        pp = q0 + t4
                        s.wait_ge(sAC, tot(sAC))
                        i = s.activation(st_nmr[:, pp:pp + 1], st_negm[:, pp:pp + 1],
                                         AF.Identity, bias=0.0, scale=st_rstd[:, pp:pp + 1])
                        inc(i, sAC, 1)
                    s.wait_ge(sAC, tot(sAC))
                    for t4 in range(4):
                        t = 4 * g + t4
                        pp = q0 + t4
                        if t >= 4:
                            s.wait_ge(sXNW, xnw_done[t - 1])
                        i = s.activation(xnbufs[t4][:], xbufs[t % 8][:], AF.Identity,
                                         bias=st_nmr[:, pp:pp + 1], scale=st_rstd[:, pp:pp + 1])
                        aff_done[t] = inc(i, sAC, 1)
                        s.wait_ge(sAC, aff_done[t])
                        i = s.dma_start(xn_sc[t * 128:(t + 1) * 128, :], xnbufs[t4][:])
                        xnw_done[t] = inc(i, sXNW, 16)

                gb = g % 2
                own_group = g < OG

                @block.scalar
                def _(sy, g=g, gb=gb, own_group=own_group):
                    if not own_group and g >= 4:
                        sy.wait_ge(sMM, mm_grp_done[g - 2])
                    sy.wait_ge(sXNW, xnw_done[4 * g + 3])
                    for f in range(FC):
                        if own_group:
                            dst = xnT_own[:, f * OWN + g * 512: f * OWN + (g + 1) * 512]
                        else:
                            dst = xnTgs[gb][:, f * 512:(f + 1) * 512]
                        i = sy.dma_start_transpose(
                            dst, xn_sc[g * 512:(g + 1) * 512, f * 128:(f + 1) * 128])
                        xnt_done[g] = inc(i, sXT[g % 2], 16)

                def rhs_of(f, g=g, gb=gb, own_group=own_group):
                    if own_group:
                        return xnT_own[:, f * OWN + g * 512: f * OWN + (g + 1) * 512]
                    return xnTgs[gb][:, f * 512:(f + 1) * 512]

                def emit_evac(ge):
                    @block.vector
                    def _(v, ge=ge):
                        if ge >= 1:
                            v.wait_ge(sCKW, 64 * ge)
                        for lc in range(LC):
                            ps = [S0[:, 0:512], S0[:, 512:1024],
                                  S1[:, 0:512], S1[:, 512:1024]][lc]
                            v.wait_ge(sMM, cnt["mm_ckv_%d_%d" % (ge, lc)])
                            i = v.tensor_copy(ckvsts[lc][:], ps)
                            cnt["ve_ckv_%d_%d" % (ge, lc)] = inc(i, sVE, 1)
                        tb = [ckg0, ckg1][ge % 2]
                        v.wait_ge(sMM, cnt["mm_krt_%d" % ge])
                        v.wait_ge(sTB[ge % 2], 16 * (ge // 2 + 1))
                        ilk(v, v.tensor_tensor(rtmpa[:], C0[0:PD, 0:512],
                                               tb[:, 0:512], ALU.mult), sVE)
                        ilk(v, v.tensor_tensor(rtmpb[:], C1[0:PD, 0:512],
                                               tb[:, 512:1024], ALU.mult), sVE)
                        i = v.tensor_add(krt_sb[:, ge * 512:(ge + 1) * 512], rtmpa[:], rtmpb[:])
                        cnt["rope_%d" % ge] = inc(i, sVE, 1)

                    @block.gpsimd
                    def _(sy, ge=ge):
                        for lc in range(LC):
                            sy.wait_ge(sVE, cnt["ve_ckv_%d_%d" % (ge, lc)])
                            i = sy.dma_start(ckv_sc[ge, :, lc * 512:(lc + 1) * 512],
                                             ckvsts[lc][:])
                            ckw_stage[lc] = inc(i, sCKW, 16)

                @block.gpsimd
                def _(gq, g=g):
                    tb = [ckg0, ckg1][g % 2]
                    if g >= 2:
                        gq.wait_ge(sVE, cnt["rope_%d" % (g - 2)])
                    i = gq.dma_start(tb[:].rearrange("p (u t) -> p u t", t=512),
                                     tabk_d[:, :, g * 512:(g + 1) * 512].rearrange("u p t -> p u t"))
                    cnt["tab_%d" % g] = inc(i, sTB[g % 2], 16)

                if g >= 1:
                    emit_evac(g - 1)

                @block.tensor
                def _(t_, g=g, rhs_of=rhs_of):
                    t_.wait_ge(sXT[g % 2], 16 * FC * (g // 2 + 1))
                    if g == 0:
                        t_.wait_ge(sW, 32)
                        t_.wait_ge(sVE, wkr_rot_done)
                    if g >= 2:
                        t_.wait_ge(sVE, cnt["rope_%d" % (g - 2)])
                    for lc in range(LC):
                        ps = [S0[:, 0:512], S0[:, 512:1024],
                              S1[:, 0:512], S1[:, 512:1024]][lc]
                        for f in range(FC):
                            i = t_.matmul(ps, wdkv_sb[:, (f * LC + lc) * 128:(f * LC + lc + 1) * 128],
                                          rhs_of(f), start=(f == 0), stop=(f == FC - 1))
                        cnt["mm_ckv_%d_%d" % (g, lc)] = inc(i, sMM, 1)
                    for f in range(FC):
                        i = t_.matmul(C0[0:PD, 0:512], wkr_sb[:, f * PD:(f + 1) * PD],
                                      rhs_of(f), start=(f == 0), stop=(f == FC - 1))
                    inc(i, sMM, 1)
                    for f in range(FC):
                        i = t_.matmul(C1[0:PD, 0:512], wkr_rot[:, f * PD:(f + 1) * PD],
                                      rhs_of(f), start=(f == 0), stop=(f == FC - 1))
                    mm_grp_done[g] = inc(i, sMM, 1)
                    cnt["mm_krt_%d" % g] = tot(sMM)

                if g == NG - 1:
                    emit_evac(g)

            ph0_mm = tot(sMM)
            ph0_ckw = tot(sCKW)
            ph0_sac = tot(sAC)
            ph0_sve = tot(sVE)
            s0_half_guard[0] = cnt["rope_%d" % (NG - 1)]
            s0_half_guard[1] = cnt["rope_%d" % (NG - 1)]
        _p0.close()

        if "krt" in dbg:
            @block.sync
            def _(sy):
                sy.wait_ge(sVE, ph0_sve)
                i = sy.dma_start(krt_dbg, krt_sb[:])
                inc(i, sOUT, 16)

        if stage == "p0":
            @block.gpsimd
            def _(g):
                g.wait_ge(sCKW, ph0_ckw)
                g.wait_ge(sXNT, tot(sXNT))
                i = g.dma_start(out_d[0:128, :], xnT_own[:, 0:2048])
                inc(i, sOUT, 16)
                g.wait_ge(sOUT, tot(sOUT))
            return nc

        _pa = contextlib.ExitStack()
        _st.enter_context(_pa)
        if True:
            EA = _pa.enter_context
            cqt_sb = EA(nc.sbuf_tensor("cqt_sb", [128, LC * OWN], BF16))
            cosq_sb = EA(nc.sbuf_tensor("cosq_sb", [PD, OWN], F32))
            sinq_sb = EA(nc.sbuf_tensor("sinq_sb", [PD, OWN], F32))
            wdqc0 = EA(nc.sbuf_tensor("wdqc0", [128, FC * 128], BF16))
            wdqc1 = EA(nc.sbuf_tensor("wdqc1", [128, FC * 128], BF16))
            wuk_sb = EA(nc.sbuf_tensor("wuk_sb", [128, LC * DM], BF16))
            wuv_sb = EA(nc.sbuf_tensor("wuv_sb", [128, LC * DM], BF16))
            wuq_sb = EA(nc.sbuf_tensor("wuq_sb", [128, LC * DM], BF16))
            wqr_sb = EA(nc.sbuf_tensor("wqr_sb", [128, LC * H * PD], BF16))
            wqr_rot = EA(nc.sbuf_tensor("wqr_rot", [128, LC * H * PD], BF16))
            kct_sb = EA(nc.sbuf_tensor("kct_sb", [128, TOK], BF16))
            vext_sb = EA(nc.sbuf_tensor("vext_sb", [128, 64 * 129], BF16))
            qct_h = EA(nc.sbuf_tensor("qct_h", [128, OWN], BF16))
            qrt_h = EA(nc.sbuf_tensor("qrt_h", [PD, OWN], BF16))
            qtmp_a = EA(nc.sbuf_tensor("qtmp_a", [PD, OWN], F32))
            qtmp_b = EA(nc.sbuf_tensor("qtmp_b", [PD, OWN], F32))
            expT0 = EA(nc.sbuf_tensor("expT0", [128, 1024], BF16))
            expT1 = EA(nc.sbuf_tensor("expT1", [128, 1024], BF16))
            ckvts = [EA(nc.sbuf_tensor("ckvt%d" % i, [128, LC * 512], BF16)) for i in range(5)]
            obuf = EA(nc.sbuf_tensor("obuf", [128, 8 * HD], BF16))
            ctx0_st = EA(nc.sbuf_tensor("ctx0_st", [128, 512], BF16)) if "ctx0" in dbg else None
            orcp = EA(nc.sbuf_tensor("orcp", [128, 1], F32))
            stab_sb = EA(nc.sbuf_tensor("stab_sb", [128, 1], F32))
            expTs = [expT0, expT1]

            @block.gpsimd
            def _(g):
                g.wait_ge(sMM, ph0_mm)
                g.wait_ge(sAC, ph0_sac)
                g.wait_ge(sVE, ph0_sve)
                g.wait_ge(sCKW, ph0_ckw)
                inc(g.dma_start(wuk_sb[:], wuk_d), sW, 16)
                inc(g.dma_start(wuv_sb[:], wuv_d), sW, 16)
                inc(g.dma_start(wuq_sb[:], wuq_d), sW, 16)
                inc(g.dma_start(cosq_sb[:], cosq_d), sW, 16)
                inc(g.dma_start(sinq_sb[:], sinq_d), sW, 16)
                cnt["w_att"] = inc(g.dma_start(wqr_sb[:], wqr_d), sW, 16)
                ones_view = vext_sb[:].rearrange("p (j c) -> p j c", c=129)
                g.memset(ones_view[:, :, 128:129], 1.0)
                i = g.memset(stab_sb[:], STAB)
                cnt["att_memset"] = inc(i, sW, 1)

            @block.vector
            def _(v):
                v.wait_ge(sW, cnt["w_att"])
                wv = wqr_sb[:].rearrange("p (c d) -> p c d", d=32)
                rv = wqr_rot[:].rearrange("p (c d) -> p c d", d=32)
                v.tensor_scalar(rv[:, 0::2, :], wv[:, 1::2, :], -1.0, None, ALU.mult)
                i = v.tensor_scalar(rv[:, 1::2, :], wv[:, 0::2, :], 1.0, None, ALU.mult)
                cnt["wqr_rot"] = inc(i, sVE, 1)

            wdqcs = [wdqc0, wdqc1]
            for lc in range(LC):
                @block.gpsimd
                def _(g, lc=lc):
                    if lc >= 2:
                        g.wait_ge(sMM, cnt["mm_cq_%d_%d" % (lc - 2, 1)])
                    i = g.dma_start(wdqcs[lc % 2][:],
                                    wdq_d[:, lc * FC * 128:(lc + 1) * FC * 128])
                    cnt["wdq_ld_%d" % lc] = inc(i, sGU[lc % 2], 16)

                for hf in range(2):
                    @block.tensor
                    def _(t_, lc=lc, hf=hf):
                        if hf == 0:
                            t_.wait_ge(sGU[lc % 2], cnt["wdq_ld_%d" % lc])
                        if lc == 0 and hf == 0:
                            t_.wait_ge(sW, cnt["w_att"])
                        if s0_half_guard[hf]:
                            t_.wait_ge(sVE, s0_half_guard[hf])
                        ps = S0[:, hf * 512:(hf + 1) * 512]
                        for f in range(FC):
                            i = t_.matmul(ps, wdqcs[lc % 2][:, f * 128:(f + 1) * 128],
                                          xnT_own[:, f * OWN + hf * 512: f * OWN + (hf + 1) * 512],
                                          start=(f == 0), stop=(f == FC - 1))
                        cnt["mm_cq_%d_%d" % (lc, hf)] = inc(i, sMM, 1)

                    @block.vector
                    def _(v, lc=lc, hf=hf):
                        v.wait_ge(sMM, cnt["mm_cq_%d_%d" % (lc, hf)])
                        i = v.tensor_copy(cqt_sb[:, lc * OWN + hf * 512: lc * OWN + (hf + 1) * 512],
                                          S0[:, hf * 512:(hf + 1) * 512])
                        s0_half_guard[hf] = inc(i, sVE, 1)
            cnt["cqt_done"] = tot(sVE)

            if "cqt" in dbg:
                @block.sync
                def _(sy):
                    sy.wait_ge(sVE, cnt["cqt_done"])
                    i = sy.dma_start(cqt_dbg, cqt_sb[:])
                    inc(i, sOUT, 16)

            A0 = tot(sAC)
            exp_done = {}
            for hh in range(nheads):
                for jj in range(64):
                    exp_done[(hh, jj)] = A0 + hh * 64 + jj + 1
            kv = {}
            o_evac_done = 0
            last_exp = 0

            def ckv_fetch(g, h, c):
                buf = ckvts[c % 5]
                prev = kv.get(("mm", h, c - 5), kv.get(("mm", h - 1, 11 + c)))
                if prev:
                    g.wait_ge(sMM, prev)
                if h == 0 and c == 0:
                    g.wait_ge(sCKW, ph0_ckw)
                i = g.dma_start(buf[:], ckv_sc[c])
                kv[("rd", h, c)] = inc(i, sCK[c % 5], 16)

            pc_jobs = []
            if precast:
                for e in range(E + NSUB):
                    for c in range(ICE):
                        pc_jobs.append((gu_bf[e, c], gu_d[e, c]))
                    for lo in range(0, ICE, 2):
                        hi = min(ICE, lo + 2)
                        pc_jobs.append((dn_bf[e, lo:hi], dn_d[e, lo:hi]))
                for q in range(8):
                    pc_jobs.append((wo_bf[:, q * 4096:(q + 1) * 4096],
                                    wo_d[:, q * 4096:(q + 1) * 4096]))
            cnt["pc_total"] = 16 * len(pc_jobs)
            cnt["pc_wo"] = 16 * len(pc_jobs)
            pc_next = [0]
            pc_slots = [0]
            pc_nslots = max(1, (H - 1) * 13)

            def emit_pc(g):
                pc_slots[0] += 1
                want = len(pc_jobs) * pc_slots[0] // pc_nslots
                while pc_next[0] < min(want, len(pc_jobs)):
                    dst, src_ = pc_jobs[pc_next[0]]
                    inc(g.dma_start(dst, src_), sPC, 16)
                    pc_next[0] += 1

            for h in range(nheads):
                @block.gpsimd
                def _(g, h=h):
                    for c in range(3):
                        ckv_fetch(g, h, c)

                @block.tensor
                def _(t_, h=h):
                    if h == 0:
                        t_.wait_ge(sVE, cnt["cqt_done"])
                        t_.wait_ge(sVE, cnt["wqr_rot"])
                    else:
                        t_.wait_ge(sAC, last_exp)
                        t_.wait_ge(sVE, o_evac_done)
                    for hf in range(2):
                        ps = S1[:, hf * 512:(hf + 1) * 512]
                        for lc in range(LC):
                            i = t_.matmul(ps, wuq_sb[:, lc * DM + h * HD: lc * DM + (h + 1) * HD],
                                          cqt_sb[:, lc * OWN + hf * 512: lc * OWN + (hf + 1) * 512],
                                          start=(lc == 0), stop=(lc == LC - 1))
                    cnt["mm_qc_%d" % h] = inc(i, sMM, 1)
                    for hf in range(2):
                        ps = S0[0:PD, hf * 512:(hf + 1) * 512]
                        for lc in range(LC):
                            i = t_.matmul(ps, wqr_sb[:, lc * H * PD + h * PD: lc * H * PD + (h + 1) * PD],
                                          cqt_sb[:, lc * OWN + hf * 512: lc * OWN + (hf + 1) * 512],
                                          start=(lc == 0), stop=(lc == LC - 1))
                    for hf in range(2):
                        ps = CT[2 + hf][0:PD, 0:512]
                        for lc in range(LC):
                            i = t_.matmul(ps, wqr_rot[:, lc * H * PD + h * PD: lc * H * PD + (h + 1) * PD],
                                          cqt_sb[:, lc * OWN + hf * 512: lc * OWN + (hf + 1) * 512],
                                          start=(lc == 0), stop=(lc == LC - 1))
                    cnt["mm_qr_%d" % h] = inc(i, sMM, 1)

                @block.vector
                def _(v, h=h):
                    v.wait_ge(sMM, cnt["mm_qc_%d" % h])
                    v.tensor_copy(qct_h[:, 0:512], S1[:, 0:512])
                    i = v.tensor_copy(qct_h[:, 512:1024], S1[:, 512:1024])
                    inc(i, sVE, 1)
                    v.wait_ge(sMM, cnt["mm_qr_%d" % h])
                    v.tensor_tensor(qtmp_a[:, 0:512], S0[0:PD, 0:512], cosq_sb[:, 0:512], ALU.mult)
                    v.tensor_tensor(qtmp_a[:, 512:1024], S0[0:PD, 512:1024], cosq_sb[:, 512:1024], ALU.mult)
                    v.tensor_tensor(qtmp_b[:, 0:512], CT[2][0:PD, 0:512], sinq_sb[:, 0:512], ALU.mult)
                    ilk(v, v.tensor_tensor(qtmp_b[:, 512:1024], CT[3][0:PD, 0:512], sinq_sb[:, 512:1024], ALU.mult), sVE)
                    i = v.tensor_add(qrt_h[:], qtmp_a[:], qtmp_b[:])
                    cnt["q_ready_%d" % h] = inc(i, sVE, 1)

                for c in range(16):
                    @block.tensor
                    def _(t_, h=h, c=c):
                        buf = ckvts[c % 5]
                        t_.wait_ge(sCK[c % 5], kv[("rd", h, c)])
                        if c == 0:
                            t_.wait_ge(sVE, cnt["q_ready_%d" % h])
                        half = c % 2
                        prevk = kv.get(("kev", h, c - 2), kv.get(("kev", h - 1, 14 + c)))
                        if prevk:
                            t_.wait_ge(sVE, prevk)
                        ps_k = S0[:, half * 512:(half + 1) * 512]
                        for lc in range(LC):
                            i = t_.matmul(ps_k, wuk_sb[:, lc * DM + h * HD: lc * DM + (h + 1) * HD],
                                          buf[:, lc * 512:(lc + 1) * 512],
                                          start=(lc == 0), stop=(lc == LC - 1))
                        kv[("kmm", h, c)] = inc(i, sMM, 1)
                        prevv = kv.get(("vev", h, c - 2), kv.get(("vev", h - 1, 14 + c)))
                        if prevv:
                            t_.wait_ge(sVE, prevv)
                        ps_v = S1[:, half * 512:(half + 1) * 512]
                        for sub in range(4):
                            pv = ps_v[:, sub * 128:(sub + 1) * 128]
                            for lc in range(LC):
                                i = t_.matmul(pv, buf[:, lc * 512 + sub * 128: lc * 512 + (sub + 1) * 128],
                                              wuv_sb[:, lc * DM + h * HD: lc * DM + (h + 1) * HD],
                                              start=(sub == 0 and lc == 0),
                                              stop=(sub == 3 and lc == LC - 1))
                        kv[("mm", h, c)] = inc(i, sMM, 1)

                    if c + 3 < 16:
                        @block.gpsimd
                        def _(g, h=h, c=c):
                            ckv_fetch(g, h, c + 3)
                            if h >= 1:
                                emit_pc(g)

                    @block.vector
                    def _(v, h=h, c=c):
                        half = c % 2
                        v.wait_ge(sMM, kv[("kmm", h, c)])
                        i = v.tensor_copy(kct_sb[:, c * 512:(c + 1) * 512],
                                          S0[:, half * 512:(half + 1) * 512])
                        kv[("kev", h, c)] = inc(i, sVE, 1)
                        v.wait_ge(sMM, kv[("mm", h, c)])
                        vv = vext_sb[:].rearrange("p (j c2) -> p j c2", c2=129)
                        i = v.tensor_copy(vv[:, 4 * c:4 * c + 4, 0:128],
                                          S1[:, half * 512:(half + 1) * 512].rearrange("p (s d) -> p s d", d=128))
                        kv[("vev", h, c)] = inc(i, sVE, 1)

                @block.tensor
                def _(t_, h=h):
                    def scores(j):
                        if j < 2:
                            t_.wait_ge(sVE, kv[("kev", h, 15)])
                            t_.wait_ge(sVE, kv[("vev", h, 15)])
                            prev_exp = exp_done.get((h - 1, 62 + j))
                            if prev_exp:
                                t_.wait_ge(sAC, prev_exp)
                        Sb = SS[j % 2]
                        for hf in range(2):
                            ps = Sb[:, hf * 512:(hf + 1) * 512]
                            t_.matmul(ps, kct_sb[:, j * 128:(j + 1) * 128],
                                      qct_h[:, hf * 512:(hf + 1) * 512], start=True, stop=False)
                            i = t_.matmul(ps, krt_sb[:, j * 128:(j + 1) * 128],
                                          qrt_h[:, hf * 512:(hf + 1) * 512], start=False, stop=True)
                        cnt["sc_%d_%d" % (h, j)] = inc(i, sMM, 1)

                    if h == 0:
                        t_.wait_ge(sW, cnt["att_memset"])
                    scores(0)
                    scores(1)
                    for j in range(64):
                        t_.wait_ge(sAC, exp_done[(h, j)])
                        if j == 0 and o_evac_done:
                            t_.wait_ge(sVE, o_evac_done)
                        for qt in range(8):
                            off = (qt % 2) * 256
                            i = t_.matmul(CT[qt // 2][:, off:off + 129],
                                          expTs[j % 2][:, qt * 128:(qt + 1) * 128],
                                          vext_sb[:, j * 129:(j + 1) * 129],
                                          start=(j == 0 and qt % 2 == 0),
                                          stop=(j == 63 and qt % 2 == 1))
                        cnt["ctx_%d_%d" % (h, j)] = inc(i, sMM, 1)
                        if j + 2 < 64:
                            scores(j + 2)

                last_exp = exp_done[(h, 63)]

                if h == 0 and "ctx0" in dbg:
                    @block.vector
                    def _(v):
                        v.wait_ge(sMM, cnt["ctx_0_63"])
                        i = v.tensor_copy(ctx0_st[:], C0[:, 0:512])
                        cnt["ctx0_ev"] = inc(i, sVE, 1)

                    @block.sync
                    def _(sy):
                        sy.wait_ge(sVE, cnt["ctx0_ev"])
                        inc(sy.dma_start(ctx0_dbg, ctx0_st[:]), sOUT, 16)

                @block.vector
                def _(v, h=h):
                    v.wait_ge(sMM, cnt["ctx_%d_%d" % (h, 63)])
                    if h > 0:
                        v.wait_ge(sOS, cnt["osw_%d" % (h - 1)])
                    for qt in range(8):
                        off = (qt % 2) * 256
                        ilk(v, v.reciprocal(orcp[:], CT[qt // 2][:, off + 128:off + 129]), sVE)
                        i = v.tensor_scalar(obuf[:, qt * HD:(qt + 1) * HD],
                                            CT[qt // 2][:, off:off + 128],
                                            orcp[:], None, ALU.mult)
                        cnt["oev_%d" % (h * 8 + qt)] = inc(i, sVE, 1)

                @block.sync
                def _(sy, h=h):
                    sy.wait_ge(sVE, cnt["oev_%d" % (h * 8 + 7)])
                    for qt in range(8):
                        i = sy.dma_start(o_sc[qt * 128:(qt + 1) * 128, h * HD:(h + 1) * HD],
                                         obuf[:, qt * HD:(qt + 1) * HD])
                        cnt["osw_%d" % h] = inc(i, sOS, 16)
                o_evac_done = cnt["oev_%d" % (h * 8 + 7)]

            @block.scalar
            def _(s):
                s.wait_ge(sW, cnt["att_memset"])
                for h in range(nheads):
                    for j in range(64):
                        s.wait_ge(sMM, cnt["sc_%d_%d" % (h, j)])
                        pc = cnt.get("ctx_%d_%d" % (h, j - 2),
                                     cnt.get("ctx_%d_%d" % (h - 1, 62 + j)))
                        if pc:
                            s.wait_ge(sMM, pc)
                        i = s.activation(expTs[j % 2][:], SS[j % 2][:], AF.Exp,
                                         bias=stab_sb[:], scale=SCALE)
                        got = inc(i, sAC, 1)
                        assert got == exp_done[(h, j)], (got, exp_done[(h, j)])

            if pc_jobs and pc_next[0] < len(pc_jobs):
                @block.gpsimd
                def _(g):
                    while pc_next[0] < len(pc_jobs):
                        dst, src_ = pc_jobs[pc_next[0]]
                        inc(g.dma_start(dst, src_), sPC, 16)
                        pc_next[0] += 1

            att_os = tot(sOS)
            att_mm = tot(sMM)
            att_sac = tot(sAC)
            att_sve = tot(sVE)

            if "kvq" in dbg:
                @block.sync
                def _(sy):
                    sy.wait_ge(sMM, att_mm)
                    sy.wait_ge(sAC, att_sac)
                    inc(sy.dma_start(kct_dbg, kct_sb[:]), sOUT, 16)
                    inc(sy.dma_start(vext_dbg, vext_sb[:]), sOUT, 16)
                    inc(sy.dma_start(qct_dbg, qct_h[:]), sOUT, 16)
                    inc(sy.dma_start(qrt_dbg, qrt_h[:]), sOUT, 16)
                    inc(sy.dma_start(expt_dbg, expTs[1][:]), sOUT, 16)
        _pa.close()
        _pk.close()

        if stage == "attn":
            @block.gpsimd
            def _(g):
                g.wait_ge(sOS, att_os)
                i = g.dma_start(out_d[0:128, :], xnT_own[:, 0:2048])
                inc(i, sOUT, 16)
                g.wait_ge(sOUT, tot(sOUT))
            return nc

        _pw = contextlib.ExitStack()
        _st.enter_context(_pw)
        if True:
            EW = _pw.enter_context
            out_acc = EW(nc.sbuf_tensor("out_acc", [128, 8 * DM], F32))
            _pw2 = contextlib.ExitStack()
            _pw.enter_context(_pw2)
            wo_sb = _pw2.enter_context(nc.sbuf_tensor("wo_sb", [128, FC * DM], BF16))
            oT_sb = _pw2.enter_context(nc.sbuf_tensor("oT_sb", [128, FC * OWN], BF16))
            xr0 = _pw2.enter_context(nc.sbuf_tensor("xr0", [128, DM], F32))

            @block.gpsimd
            def _(g):
                g.wait_ge(sMM, att_mm)
                g.wait_ge(sAC, att_sac)
                g.wait_ge(sVE, att_sve)
                if cnt["pc_wo"]:
                    g.wait_ge(sPC, cnt["pc_wo"])
                    i = g.dma_start(wo_sb[:], wo_bf)
                    cnt["wo_ld"] = inc(i, sW, 16)
                else:
                    for f in range(FC):
                        i = g.dma_start(wo_sb[:, f * DM:(f + 1) * DM],
                                        wo_d[:, f * DM:(f + 1) * DM])
                        cnt["wo_ld"] = inc(i, sW, 16)

            @block.scalar
            def _(s):
                s.wait_ge(sOS, att_os)
                s.wait_ge(sMM, att_mm)
                for f in range(FC):
                    i = s.dma_start_transpose(oT_sb[:, f * OWN:(f + 1) * OWN],
                                              o_sc[:, f * 128:(f + 1) * 128])
                    cnt["oT_ld"] = inc(i, sOT, 16)

            if "oacc" in dbg:
                @block.sync
                def _(sy):
                    sy.wait_ge(sOT, cnt["oT_ld"])
                    inc(sy.dma_start(ot2_dbg, oT_sb[:]), sOUT, 16)

            wo_mm = {}
            for tt in range(8):
                @block.sync
                def _(sy, tt=tt):
                    if tt >= 1:
                        sy.wait_ge(sVE, cnt["woev_%d_%d" % (tt - 1, 3)])
                    i = sy.dma_start(xr0[:], x_d[tt * 128:(tt + 1) * 128, :])
                    cnt["xr_%d" % tt] = inc(i, sX, 16)

                for nn in range(4):
                    @block.tensor
                    def _(t_, tt=tt, nn=nn):
                        if tt == 0 and nn == 0:
                            t_.wait_ge(sW, cnt["wo_ld"])
                            t_.wait_ge(sOT, cnt["oT_ld"])
                            t_.wait_ge(sAC, att_sac)
                        half = nn % 2
                        prev = cnt.get("woev_%d_%d" % (tt, nn - 2),
                                       cnt.get("woev_%d_%d" % (tt - 1, nn + 2)))
                        if prev:
                            t_.wait_ge(sVE, prev)
                        ps = S0[:, half * 512:(half + 1) * 512]
                        for f in range(FC):
                            i = t_.matmul(ps, oT_sb[:, f * OWN + tt * 128: f * OWN + (tt + 1) * 128],
                                          wo_sb[:, f * DM + nn * 512: f * DM + (nn + 1) * 512],
                                          start=(f == 0), stop=(f == FC - 1))
                        wo_mm[(tt, nn)] = inc(i, sMM, 1)

                    @block.vector
                    def _(v, tt=tt, nn=nn):
                        if nn == 0:
                            v.wait_ge(sX, cnt["xr_%d" % tt])
                        half = nn % 2
                        v.wait_ge(sMM, wo_mm[(tt, nn)])
                        i = v.tensor_add(out_acc[:, tt * DM + nn * 512: tt * DM + (nn + 1) * 512],
                                         S0[:, half * 512:(half + 1) * 512],
                                         xr0[:, nn * 512:(nn + 1) * 512])
                        cnt["woev_%d_%d" % (tt, nn)] = inc(i, sVE, 1)
            cnt["wo_done"] = tot(sVE)

            if "oacc" in dbg:
                @block.sync
                def _(sy):
                    sy.wait_ge(sVE, cnt["wo_done"])
                    inc(sy.dma_start(oacc_dbg, out_acc[:]), sOUT, 16)
                    inc(sy.dma_start(ot_dbg, oT_sb[:]), sOUT, 16)
                    inc(sy.dma_start(wosb_dbg, wo_sb[:]), sOUT, 16)
                    inc(sy.dma_start(xr_dbg, xr0[:]), sOUT, 16)

            if stage == "wo":
                @block.sync
                def _(sy):
                    sy.wait_ge(sVE, cnt["wo_done"])
                    for tt in range(8):
                        i = sy.dma_start(out_d[tt * 128:(tt + 1) * 128, :],
                                         out_acc[:, tt * DM:(tt + 1) * DM])
                        inc(i, sOUT, 16)
                    sy.wait_ge(sOUT, tot(sOUT))
                return nc

            _pw2.close()
            EF = _pw.enter_context
            rtr_sb = EF(nc.sbuf_tensor("rtr_sb", [128, FC * E], BF16))
            sgw_sb = EF(nc.sbuf_tensor("sgw_sb", [128, FC], BF16))
            logit_sb = EF(nc.sbuf_tensor("logit_sb", [128, 8 * E], F32))
            comb_sb = EF(nc.sbuf_tensor("comb_sb", [128, 8 * E], F32))
            sg_sb = EF(nc.sbuf_tensor("sg_sb", [128, 8], F32))
            diff_sb = EF(nc.sbuf_tensor("diff_sb", [128, 8], F32))
            c1_sb = EF(nc.sbuf_tensor("c1_sb", [128, 8], F32))
            m1_sb = EF(nc.sbuf_tensor("m1_sb", [128, 1], F32))
            m2_sb = EF(nc.sbuf_tensor("m2_sb", [128, 1], F32))
            msk1 = EF(nc.sbuf_tensor("msk1", [128, E], F32))
            msk2 = EF(nc.sbuf_tensor("msk2", [128, E], F32))
            lgt = EF(nc.sbuf_tensor("lgt", [128, E], F32))
            gub0 = EF(nc.sbuf_tensor("gub0", [128, 2 * DM], BF16))
            gub1 = EF(nc.sbuf_tensor("gub1", [128, 2 * DM], BF16))
            dnt_sb = EF(nc.sbuf_tensor("dnt_sb", [128, ICE * DM], BF16))
            hT_sb = EF(nc.sbuf_tensor("hT_sb", [128, ICE * OWN], BF16))
            silu_sb = EF(nc.sbuf_tensor("silu_sb", [128, 1024], F32))
            if True:
                gubs = [gub0, gub1]

                @block.gpsimd
                def _(g):
                    g.wait_ge(sMM, att_mm)
                    g.wait_ge(sAC, att_sac)
                    g.wait_ge(sVE, cnt["wo_done"])
                    inc(g.dma_start(rtr_sb[:], rtr_d), sW, 16)
                    cnt["rtr_ld"] = inc(g.dma_start(sgw_sb[:], sgw_d), sW, 16)

                for tt in range(8):
                    @block.tensor
                    def _(t_, tt=tt):
                        if tt == 0:
                            t_.wait_ge(sW, cnt["rtr_ld"])
                            t_.wait_ge(sVE, cnt["wo_done"])
                            t_.wait_ge(sAC, att_sac)
                        else:
                            t_.wait_ge(sVE, cnt["rt_ev_%d" % (tt - 1)])
                        ps = C0[:, 0:E]
                        for f in range(FC):
                            t_.matmul(ps, xnT_own[:, f * OWN + tt * 128: f * OWN + (tt + 1) * 128],
                                      rtr_sb[:, f * E:(f + 1) * E],
                                      start=(f == 0), stop=(f == FC - 1))
                        ps2 = C1[:, 0:1]
                        for f in range(FC):
                            i2 = t_.matmul(ps2, xnT_own[:, f * OWN + tt * 128: f * OWN + (tt + 1) * 128],
                                           sgw_sb[:, f:f + 1],
                                           start=(f == 0), stop=(f == FC - 1))
                        cnt["rt_mm_%d" % tt] = inc(i2, sMM, 1)

                    @block.vector
                    def _(v, tt=tt):
                        v.wait_ge(sMM, cnt["rt_mm_%d" % tt])
                        v.tensor_copy(logit_sb[:, tt * E:(tt + 1) * E], C0[:, 0:E])
                        i = v.tensor_copy(sg_sb[:, tt:tt + 1], C1[:, 0:1])
                        cnt["rt_ev_%d" % tt] = inc(i, sVE, 1)

                @block.vector
                def _(v):
                    for tt in range(8):
                        L = logit_sb[:, tt * E:(tt + 1) * E]
                        ilk(v, v.reduce_max(m1_sb[:], L, axis=X), sVE)
                        ilk(v, v.tensor_scalar(msk1[:], L, m1_sb[:], None, ALU.is_equal), sVE)
                        ilk(v, v.scalar_tensor_tensor(lgt[:], msk1[:], -1e30, L, ALU.mult, ALU.add), sVE)
                        ilk(v, v.reduce_max(m2_sb[:], lgt[:], axis=X), sVE)
                        ilk(v, v.tensor_scalar(msk2[:], lgt[:], m2_sb[:], None, ALU.is_equal), sVE)
                        ilk(v, v.tensor_sub(diff_sb[:, tt:tt + 1], m2_sb[:], m1_sb[:]), sVE)
                        v.tensor_copy(comb_sb[:, tt * E:(tt + 1) * E], msk1[:])
                        i = v.tensor_copy(logit_sb[:, tt * E:(tt + 1) * E], msk2[:])
                        cnt["rt_msk_%d" % tt] = inc(i, sVE, 1)

                @block.scalar
                def _(s):
                    s.wait_ge(sVE, cnt["rt_msk_7"])
                    i = s.activation(c1_sb[:], diff_sb[:], AF.Sigmoid)
                    inc(i, sAC, 1)
                    i = s.activation(sg_sb[:], sg_sb[:], AF.Sigmoid)
                    cnt["sig_done"] = inc(i, sAC, 1)

                @block.vector
                def _(v):
                    v.wait_ge(sAC, cnt["sig_done"])
                    for tt in range(8):
                        c2 = c1_sb[:, tt:tt + 1]
                        ilk(v, v.tensor_scalar(m1_sb[:], c2, -1.0, 1.0, ALU.mult, ALU.add), sVE)
                        ilk(v, v.tensor_scalar(msk1[:], comb_sb[:, tt * E:(tt + 1) * E], m1_sb[:], None, ALU.mult), sVE)
                        ilk(v, v.tensor_scalar(msk2[:], logit_sb[:, tt * E:(tt + 1) * E], c2, None, ALU.mult), sVE)
                        i = v.tensor_add(comb_sb[:, tt * E:(tt + 1) * E], msk1[:], msk2[:])
                        inc(i, sVE, 1)
                    cnt["comb_done"] = tot(sVE)

                if "comb" in dbg:
                    @block.sync
                    def _(sy):
                        sy.wait_ge(sVE, cnt["comb_done"])
                        i = sy.dma_start(comb_dbg, comb_sb[:])
                        inc(i, sOUT, 16)
                        i = sy.dma_start(sg_dbg, sg_sb[:])
                        inc(i, sOUT, 16)

                def expert(idx, gu_src, dn_src, comb_col):
                    @block.gpsimd
                    def _(g, idx=idx):
                        prev = cnt.get("dn_used_%d" % (idx - 1))
                        if prev:
                            g.wait_ge(sMM, prev)
                        if idx == 0 and cnt["pc_total"]:
                            g.wait_ge(sPC, cnt["pc_total"])
                        for half in range(2):
                            lo = half * 6
                            hi = min(ICE, lo + 6)
                            i = g.dma_start(
                                dnt_sb[:, lo * DM:hi * DM].rearrange("p (c f) -> p c f", f=DM),
                                dn_src(lo, hi).rearrange("c p f -> p c f"))
                            cnt["dn_ld_%d" % idx] = inc(i, sDN, 16)

                    for c in range(ICE):
                        slot = c % 2

                        @block.gpsimd
                        def _(g, idx=idx, c=c, slot=slot):
                            prev = cnt.get("gu_used_%d_%d" % (idx, c - 2),
                                           cnt.get("gu_used_%d_%d" % (idx - 1, ICE - 2 + c)))
                            if prev:
                                g.wait_ge(sMM, prev)
                            i = g.dma_start(
                                gubs[slot][:].rearrange("p (u f) -> p u f", f=DM),
                                gu_src(c).rearrange("u p f -> p u f"))
                            cnt["gu_ld_%d" % idx + "_%d" % c] = inc(i, sGU[slot], 16)

                        @block.tensor
                        def _(t_, idx=idx, c=c, slot=slot):
                            t_.wait_ge(sGU[slot], cnt["gu_ld_%d_%d" % (idx, c)])
                            for part, SP in ((0, S0), (1, S1)):
                                if part == 0:
                                    prev = cnt.get("sil_%d_%d" % (idx, c - 1),
                                                   cnt.get("sil_%d_%d" % (idx - 1, ICE - 1)))
                                    if prev:
                                        t_.wait_ge(sAC, prev)
                                else:
                                    prev = cnt.get("hev_%d_%d" % (idx, c - 1),
                                                   cnt.get("hev_%d_%d" % (idx - 1, ICE - 1)))
                                    if prev:
                                        t_.wait_ge(sVE, prev)
                                for hf in range(2):
                                    ps = SP[:, hf * 512:(hf + 1) * 512]
                                    for f in range(FC):
                                        i = t_.matmul(
                                            ps, gubs[slot][:, part * DM + f * 128: part * DM + (f + 1) * 128],
                                            xnT_own[:, f * OWN + hf * 512: f * OWN + (hf + 1) * 512],
                                            start=(f == 0), stop=(f == FC - 1))
                                inc(i, sMM, 1)
                            cnt["gu_used_%d_%d" % (idx, c)] = tot(sMM)
                            cnt["gu_mm_%d_%d" % (idx, c)] = tot(sMM)

                        @block.scalar
                        def _(s, idx=idx, c=c):
                            s.wait_ge(sMM, cnt["gu_mm_%d_%d" % (idx, c)])
                            prev = cnt.get("hmul_%d_%d_1" % (idx, c - 1),
                                           cnt.get("hmul_%d_%d_1" % (idx - 1, ICE - 1)))
                            if prev:
                                s.wait_ge(sVE, prev)
                            for hf in range(2):
                                i = s.activation(silu_sb[:, hf * 512:(hf + 1) * 512],
                                                 S0[:, hf * 512:(hf + 1) * 512], AF.Silu)
                                cnt["sil_%d_%d_%d" % (idx, c, hf)] = inc(i, sAC, 1)
                            cnt["sil_%d_%d" % (idx, c)] = tot(sAC)

                        @block.vector
                        def _(v, idx=idx, c=c):
                            for hf in range(2):
                                v.wait_ge(sAC, cnt["sil_%d_%d_%d" % (idx, c, hf)])
                                i = v.tensor_tensor(
                                    hT_sb[:, c * OWN + hf * 512: c * OWN + (hf + 1) * 512],
                                    silu_sb[:, hf * 512:(hf + 1) * 512],
                                    S1[:, hf * 512:(hf + 1) * 512], ALU.mult)
                                cnt["hmul_%d_%d_%d" % (idx, c, hf)] = inc(i, sVE, 1)
                            cnt["hev_%d_%d" % (idx, c)] = tot(sVE)

                    @block.tensor
                    def _(t_, idx=idx):
                        t_.wait_ge(sDN, cnt["dn_ld_%d" % idx])
                        t_.wait_ge(sVE, cnt["hev_%d_%d" % (idx, ICE - 1)])
                        for tt in range(8):
                            for nn in range(4):
                                k = tt * 4 + nn
                                ps = CT[k % 2][:, 0:512]
                                prev = cnt.get("dn_ev_%d_%d" % (idx, k - 2),
                                               cnt.get("dn_ev_%d_%d" % (idx - 1, 30 + (k % 2))))
                                if prev:
                                    t_.wait_ge(sVE, prev)
                                for ic in range(ICE):
                                    i = t_.matmul(ps, hT_sb[:, ic * OWN + tt * 128: ic * OWN + (tt + 1) * 128],
                                                  dnt_sb[:, ic * DM + nn * 512: ic * DM + (nn + 1) * 512],
                                                  start=(ic == 0), stop=(ic == ICE - 1))
                                cnt["dn_mm_%d_%d" % (idx, k)] = inc(i, sMM, 1)
                        cnt["dn_used_%d" % idx] = tot(sMM)

                    @block.vector
                    def _(v, idx=idx, comb_col=comb_col):
                        for tt in range(8):
                            for nn in range(4):
                                k = tt * 4 + nn
                                v.wait_ge(sMM, cnt["dn_mm_%d_%d" % (idx, k)])
                                i = v.scalar_tensor_tensor(
                                    out_acc[:, tt * DM + nn * 512: tt * DM + (nn + 1) * 512],
                                    CT[k % 2][:, 0:512], comb_col(tt),
                                    out_acc[:, tt * DM + nn * 512: tt * DM + (nn + 1) * 512],
                                    ALU.mult, ALU.add)
                                cnt["dn_ev_%d_%d" % (idx, k)] = inc(i, sVE, 1)
                        cnt["exp_done_%d" % idx] = tot(sVE)

                gu_w = gu_bf if cnt["pc_total"] else gu_d
                dn_w = dn_bf if cnt["pc_total"] else dn_d
                for e in range(nexp):
                    expert(e,
                           gu_src=lambda c, e=e: gu_w[e, c, :, :, :],
                           dn_src=lambda lo, hi, e=e: dn_w[e, lo:hi, :, :],
                           comb_col=(lambda tt, e=e: comb_sb[:, tt * E + e: tt * E + e + 1])
                                    if e < E else (lambda tt: sg_sb[:, tt:tt + 1]))

                if "ffn0" in dbg:
                    @block.sync
                    def _(sy):
                        sy.wait_ge(sVE, cnt["exp_done_%d" % (nexp - 1)] if nexp else cnt["comb_done"])
                        inc(sy.dma_start(ht_dbg, hT_sb[:]), sOUT, 16)
                        inc(sy.dma_start(gub_dbg[0], gub0[:]), sOUT, 16)
                        inc(sy.dma_start(gub_dbg[1], gub1[:]), sOUT, 16)
                        inc(sy.dma_start(dnt_dbg, dnt_sb[:]), sOUT, 16)
                        inc(sy.dma_start(silu_dbg, silu_sb[:]), sOUT, 16)

                @block.sync
                def _(sy):
                    sy.wait_ge(sVE, cnt["exp_done_%d" % (nexp - 1)] if nexp else cnt["comb_done"])
                    for tt in range(8):
                        i = sy.dma_start(out_d[tt * 128:(tt + 1) * 128, :],
                                         out_acc[:, tt * DM:(tt + 1) * DM])
                        inc(i, sOUT, 16)
                    sy.wait_ge(sOUT, tot(sOUT))
    return nc


def _rope_tables():
    inv_freq = 1.0 / 10000.0 ** (np.arange(0, PD, 2, dtype=np.float32) / PD)
    t = np.arange(TOK, dtype=np.float32)
    freqs = t[:, None] * inv_freq[None, :]
    emb = np.concatenate([freqs, freqs], -1)
    return np.cos(emb).astype(np.float32), np.sin(emb).astype(np.float32)


def prep_inputs(
    L0_attn_hidden_states, L0_attn_W_DKV, L0_attn_W_DQ, L0_attn_W_UK,
    L0_attn_W_UV, L0_attn_W_UQ, L0_attn_W_KR, L0_attn_W_QR, L0_attn_W_O,
    L0_ffn_router_weight, L0_ffn_gate_up_proj, L0_ffn_down_proj,
    L0_ffn_sh_gate_w, L0_ffn_sh_up_w, L0_ffn_sh_down_w, L0_ffn_shared_gate_w,
):
    f32 = lambda a: np.ascontiguousarray(np.asarray(a, dtype=np.float32))
    x = f32(L0_attn_hidden_states).reshape(TOK, DM)
    cos, sin = _rope_tables()

    wdkv = f32(L0_attn_W_DKV).reshape(FC, 128, LC, 128).transpose(1, 0, 2, 3).reshape(128, FC * LC * 128)
    wdq = f32(L0_attn_W_DQ).reshape(FC, 128, LC, 128).transpose(1, 2, 0, 3).reshape(128, FC * LC * 128)
    wkr = f32(L0_attn_W_KR).reshape(FC, 128, PD).transpose(1, 0, 2).reshape(128, FC * PD)
    wuk = f32(L0_attn_W_UK).reshape(LC, 128, DM).transpose(1, 0, 2).reshape(128, LC * DM)
    wuv = f32(L0_attn_W_UV).reshape(LC, 128, DM).transpose(1, 0, 2).reshape(128, LC * DM)
    wuq = f32(L0_attn_W_UQ).reshape(LC, 128, DM).transpose(1, 0, 2).reshape(128, LC * DM)
    wqr = f32(L0_attn_W_QR).reshape(LC, 128, H * PD).transpose(1, 0, 2).reshape(128, LC * H * PD)
    wo = f32(L0_attn_W_O).reshape(FC, 128, DM).transpose(1, 0, 2).reshape(128, FC * DM)
    rtr = f32(L0_ffn_router_weight).T.reshape(FC, 128, E).transpose(1, 0, 2).reshape(128, FC * E)
    sgw = f32(L0_ffn_shared_gate_w).reshape(FC, 128).T.copy()
    gu_e = f32(L0_ffn_gate_up_proj).reshape(E, 2 * ICE, 128, FC, 128).transpose(0, 1, 4, 3, 2).reshape(E, 2 * ICE, 128, DM)
    dn_e = f32(L0_ffn_down_proj).transpose(0, 2, 1).reshape(E, ICE, 128, DM)
    shg = f32(L0_ffn_sh_gate_w).reshape(NSUB * ICE, 128, FC, 128).transpose(0, 3, 2, 1).reshape(NSUB * ICE, 128, DM)
    shu = f32(L0_ffn_sh_up_w).reshape(NSUB * ICE, 128, FC, 128).transpose(0, 3, 2, 1).reshape(NSUB * ICE, 128, DM)
    shd = f32(L0_ffn_sh_down_w).T.reshape(NSUB * ICE, 128, DM).copy()
    gu = np.empty((E + NSUB, ICE, 2, 128, DM), np.float32)
    gu[:E, :, 0] = gu_e[:, :ICE]
    gu[:E, :, 1] = gu_e[:, ICE:]
    gu[E:, :, 0] = shg.reshape(NSUB, ICE, 128, DM)
    gu[E:, :, 1] = shu.reshape(NSUB, ICE, 128, DM)
    dn = np.concatenate([dn_e, shd.reshape(NSUB, ICE, 128, DM)], axis=0)

    shared = dict(wdkv=wdkv, wdq=wdq, wkr=wkr, wuk=wuk, wuv=wuv, wuq=wuq,
                  wqr=wqr, wo=wo, rtr=rtr, sgw=sgw, gu=gu, dn=dn)
    shared = {k: np.ascontiguousarray(v) for k, v in shared.items()}

    in_maps = []
    for c in range(NCORE):
        own0 = c * OWN
        m = dict(shared)
        m["x"] = np.ascontiguousarray(np.roll(x, -own0, axis=0))
        cosr = np.roll(cos, -own0, axis=0)
        sinr = np.roll(sin, -own0, axis=0)
        m["tabk"] = np.ascontiguousarray(np.stack([cosr.T, sinr.T], axis=0))
        m["cosq"] = np.ascontiguousarray(cosr[:OWN].T)
        m["sinq"] = np.ascontiguousarray(sinr[:OWN].T)
        in_maps.append(m)
    return in_maps


def kernel(**inputs):
    nc = build_program()
    in_maps = prep_inputs(**inputs)
    res = run_bass_kernel_spmd(nc, in_maps, core_ids=list(range(NCORE)))
    out = np.concatenate([res.results[c]["out"] for c in range(NCORE)], axis=0)
    return out.reshape(1, TOK, DM).astype(np.float32)
